# version 08
# baseline (speedup 1.0000x reference)
import numpy as np
import ml_dtypes
from contextlib import ExitStack

import concourse.bass as bass
import concourse.mybir as mybir
from concourse.bass_utils import run_bass_kernel_spmd

F32 = mybir.dt.float32
BF16 = mybir.dt.bfloat16
AF = mybir.ActivationFunctionType
ALU = mybir.AluOpType
AX = mybir.AxisListType

D = 1024
NCH_F = 8
E = 8
KQ = 64
DIN = 2048
QC = 16
NS = 16
EPS_LN = 1e-5
ALL_PHASES = ("l0f1", "l0a", "l0f2", "l1f1", "l1a", "l1f2")


class Sem:
    def __init__(self, h):
        self.h = h
        self.n = 0


def build_program(T=2048, phases=ALL_PHASES):
    TT = min(512, T)
    NTT = T // TT
    TCH = TT
    NCHK = NTT
    HALF = TT // 2
    NTI = T // 128

    nc = bass.Bass("TRN2", target_bir_lowering=False)

    def din(name, shape, dt=F32):
        return nc.dram_tensor(name, list(shape), dt, kind="ExternalInput").ap()

    x_h = din("x", [128, NCH_F, T])
    out_h = nc.dram_tensor("out", [128, NCH_F, T], F32, kind="ExternalOutput").ap()

    wq_h = din("l0a_wq", [128, NCH_F, KQ])
    wk_h = din("l0a_wk", [128, NCH_F, KQ])
    wv_h = din("l0a_wv", [128, NCH_F, KQ])
    wo_h = din("l0a_wo", [KQ, D])
    g0_h = din("l0f_gate", [128, NCH_F, E])
    wg0_h = din("l0f_wg", [E, 128, NCH_F, 512])
    wu0_h = din("l0f_wu", [E, 128, NCH_F, 512])
    wd0_h = din("l0f_wd", [E, 128, 4, D])
    win_h = din("l1a_win", [128, NCH_F, 2 * DIN])
    convw_h = din("l1a_convw", [128, QC, 4])
    convb_h = din("l1a_convb", [128, QC])
    wx_h = din("l1a_wx", [128, QC, 96])
    wdt_h = din("l1a_wdt", [64, DIN])
    bdt_h = din("l1a_bdt", [128, QC])
    alog_h = din("l1a_alog", [128, QC, NS])
    dpar_h = din("l1a_dpar", [128, QC])
    wout_h = din("l1a_wout", [128, QC, D])
    g1_h = din("l1f_gate", [128, NCH_F, E])
    cb_h = din("l1f_cbias", [128, E])
    wg1_h = din("l1f_wg", [E, 128, NCH_F, 512])
    wu1_h = din("l1f_wu", [E, 128, NCH_F, 512])
    wd1_h = din("l1f_wd", [E, 128, 4, D])
    wgs_h = din("l1f_wgs", [128, NCH_F, 512])
    wus_h = din("l1f_wus", [128, NCH_F, 512])
    wds_h = din("l1f_wds", [128, 4, D])
    onescol_h = din("c_onescol", [128, 1], BF16)
    onedcol_h = din("c_onedcol", [128, 1], BF16)
    onesrow_h = din("c_onesrow", [1, 128], BF16)
    idbf_h = din("c_idbf", [128, 128], BF16)
    causal_h = din("c_causalT", [128, 128], BF16)
    negeps_h = din("c_negeps", [128, E], F32)
    esel8_h = din("c_esel8", [E, E * 128], BF16)
    nsel_h = din("c_nsel", [128, 32 * 128], BF16)

    wc_l0 = nc.dram_tensor("wc_l0", [24, 128, 4096], BF16, kind="Internal").ap()
    wc_l1 = nc.dram_tensor("wc_l1", [27, 128, 4096], BF16, kind="Internal").ap()
    wc_win = nc.dram_tensor("wc_win", [8, 128, 4096], BF16, kind="Internal").ap()
    wc_wout = nc.dram_tensor("wc_wout", [4, 128, 4096], BF16, kind="Internal").ap()

    with ExitStack() as ctx:
        def sbuf(name, shape, dt):
            return ctx.enter_context(nc.sbuf_tensor(name, list(shape), dt))

        RES = sbuf("RES", [128, NCH_F, T], F32)
        A1 = sbuf("A1", [128, 16384], BF16)
        A2 = sbuf("A2", [128, 16384], BF16)
        A3 = sbuf("A3", [128, 8192], BF16)
        A4 = sbuf("A4", [128, 8192], BF16)
        A5 = sbuf("A5", [128, 8448], BF16)
        A6 = sbuf("A6", [128, 2560], F32)

        gate0_s = sbuf("gate0_s", [128, NCH_F, E], BF16)
        gate1_s = sbuf("gate1_s", [128, NCH_F, E], BF16)
        cbias_s = sbuf("cbias_s", [128, E], F32)
        negeps_s = sbuf("negeps_s", [128, E], F32)
        wx_s = sbuf("wx_s", [128, QC, 96], BF16)
        wdt_s = sbuf("wdt_s", [64, DIN], BF16)
        convw_s = sbuf("convw_s", [128, QC, 4], F32)
        convb_s = sbuf("convb_s", [128, QC], F32)
        bdt_s = sbuf("bdt_s", [128, QC], F32)
        dpar_s = sbuf("dpar_s", [128, QC], F32)
        halfb_s = sbuf("halfb_s", [128, QC], F32)
        alog_s = sbuf("alog_s", [128, QC, NS], F32)
        htail_s = sbuf("htail_s", [128, QC, NS], BF16)
        halo_s = sbuf("halo_s", [128, QC, 3], BF16)
        ones_col = sbuf("ones_col", [128, 1], BF16)
        oned_col = sbuf("oned_col", [128, 1], BF16)
        ones_row = sbuf("ones_row", [1, 128], BF16)
        id_bf = sbuf("id_bf", [128, 128], BF16)
        causalT = sbuf("causalT", [128, 128], BF16)
        esel8_s = sbuf("esel8_s", [E, E * 128], BF16)
        nsel_s = sbuf("nsel_s", [128, 32 * 128], BF16)
        R_M = [A6[0:1, 0:TT], A6[0:1, TT:2 * TT]]
        R_A = A6[0:1, 2 * TT:3 * TT]
        RSB = [A6[0:1, 3 * TT:4 * TT].bitcast(BF16),
               A6[0:1, 4 * TT:5 * TT].bitcast(BF16)]
        scale_fm = A6[0:E, 2 * TT:2 * TT + T // 2].bitcast(BF16)
        xdbl_bf = A6[:, 2048:2304].bitcast(BF16)
        dtbf_s = A6[:, 1024:1280].bitcast(BF16)[:, 0:TCH]
        dtx_s = A6[:, 1280:1536].bitcast(BF16)[:, 0:TCH]
        A3P = A6[:, 1536:1792].rearrange("p (q n) -> p q n", q=QC)
        A2P = alog_s
        fixa = sbuf("fixa", [128, NS], F32)
        fixb = sbuf("fixb", [128, NS], F32)
        fixc = sbuf("fixc", [128, NS], F32)
        sst_s = sbuf("sst_s", [KQ, 2, KQ], F32)
        sbf_s = sbuf("sbf_s", [KQ, KQ], BF16)
        zst_s = sbuf("zst_s", [KQ, 2, 1], F32)
        zbf_s = sbuf("zbf_s", [KQ, 1], BF16)
        atm_s = sbuf("atm_s", [128, 128], BF16)
        otok_s = A6[:, 2304:2336].bitcast(BF16)
        den_s = A6[:, 2336:2337]
        rcp_s = A6[:, 2337:2338]

        pb = [ctx.enter_context(nc.psum_tensor(f"pb{i}", [128, 512], F32))
              for i in range(8)]
        pbf = [p[:].bitcast(BF16) for p in pb]

        WT = Sem(ctx.enter_context(nc.semaphore("wt")))
        WS0 = Sem(ctx.enter_context(nc.semaphore("ws0")))
        WS1 = Sem(ctx.enter_context(nc.semaphore("ws1")))
        WO0 = Sem(ctx.enter_context(nc.semaphore("wo0")))
        WO1 = Sem(ctx.enter_context(nc.semaphore("wo1")))
        WSEM = [WS0, WS1]
        WOSEM = [WO0, WO1]
        YW0 = Sem(ctx.enter_context(nc.semaphore("yw0")))
        YW1 = Sem(ctx.enter_context(nc.semaphore("yw1")))
        YWSEM = [YW0, YW1]
        WBSEMS = {k: Sem(ctx.enter_context(nc.semaphore(f"wb{i}")))
                  for i, k in enumerate([("ws", 0), ("ws", 1),
                                         ("wo", 0), ("wo", 1)])}
        IO = Sem(ctx.enter_context(nc.semaphore("io")))
        ST = Sem(ctx.enter_context(nc.semaphore("st")))
        SV = Sem(ctx.enter_context(nc.semaphore("sv")))
        SS = Sem(ctx.enter_context(nc.semaphore("ss")))

        Q = {e: [] for e in ("t", "v", "s", "g", "y")}

        SELF_SEM = {"v": SV, "s": SS}

        def emit(eng, fn, wait=(), inc=()):
            for s, v in wait:
                assert v <= s.n, f"wait on future value {v} > {s.n} ({eng})"
            wait = list(wait)
            own = SELF_SEM.get(eng)
            if own is not None and own.n > 0:
                wait.append((own, own.n))
            waits = {}
            for s, v in wait:
                if v > 0:
                    waits[s.h] = max(waits.get(s.h, 0), int(v))
            waits = list(waits.items())
            incs = []
            for s, dn in inc:
                s.n += dn
                incs.append((s.h, dn))

            def run(e, fn=fn, waits=waits, incs=incs):
                for h, v in waits:
                    e.wait_ge(h, v)
                r = fn(e)
                for h, dn in incs:
                    r.then_inc(h, dn)

            Q[eng].append(run)

        def t_op(fn, wait=()):
            emit("t", fn, wait=wait, inc=[(ST, 1)])
            return ST.n

        def v_op(fn, wait=()):
            emit("v", fn, wait=wait, inc=[(SV, 1)])
            return SV.n

        def s_op(fn, wait=()):
            emit("s", fn, wait=wait, inc=[(SS, 1)])
            return SS.n

        def gdma(out_ap, in_ap, wait=(), sem=None):
            sem = WT if sem is None else sem
            emit("g", lambda g, o=out_ap, i=in_ap: g.dma_start(o, i),
                 wait=wait, inc=[(sem, 16)])
            return sem.n

        def ydma(out_ap, in_ap, wait=(), sem=None):
            sem = IO if sem is None else sem
            emit("y", lambda y, o=out_ap, i=in_ap: y.dma_start(o, i),
                 wait=wait, inc=[(sem, 16)])
            return sem.n

        def mm_group(ps_ap, pairs, wait=(), start=True, stop=True, split=False):
            def fn(t, ps=ps_ap, prs=list(pairs), st0=start, sp0=stop, sp=split):
                r = None
                n = len(prs)
                if not sp:
                    for i, (lh, rh) in enumerate(prs):
                        r = t.matmul(ps, lh, rh, start=(st0 and i == 0),
                                     stop=(sp0 and i == n - 1),
                                     skip_group_check=True)
                    return r
                H = prs[0][1].shape[-1] // 2
                for i, (lh, rh) in enumerate(prs):
                    r = t.matmul(ps[:, 0:H], lh, rh[:, 0:H],
                                 start=(st0 and i == 0), stop=False,
                                 skip_group_check=True)
                    r = t.matmul(ps[:, H:2 * H], lh, rh[:, H:2 * H],
                                 start=False,
                                 stop=(sp0 and i == n - 1),
                                 skip_group_check=True)
                return r
            emit("t", fn, wait=wait, inc=[(ST, 1)])
            return ST.n

        XN = A2[:, 0:NCH_F * T].rearrange("p (c t) -> p c t", c=NCH_F)
        SQ2 = [A1[:, 0:NCH_F * TT].rearrange("p (c t) -> p c t", c=NCH_F),
               A1[:, NCH_F * TT:2 * NCH_F * TT].rearrange("p (c t) -> p c t",
                                                          c=NCH_F)]
        PSTORE = A1[:, 0:E * 4 * TT].rearrange("p (e i t) -> p e i t", e=E, i=4)
        XS_E = A4[:, 0:NCH_F * TT].rearrange("p (c t) -> p c t", c=NCH_F)
        PS_SH = A4[:, NCH_F * TT:NCH_F * TT + 4 * TT].rearrange(
            "p (i t) -> p i t", i=4)
        WSLOT = [A3[:, 0:4096], A3[:, 4096:8192]]
        SCREP = [A5[:, 0:TT], A5[:, TT:2 * TT]]
        SILUB = [A5[:, 2 * TT:3 * TT], A5[:, 3 * TT:4 * TT]]
        PTMP = A5[:, 4 * TT:5 * TT]
        _cur = [5 * TT]
        def rtv(sz_bf, dt=F32):
            off = _cur[0]
            _cur[0] += sz_bf
            v = A5[:, off:off + sz_bf]
            return v.bitcast(dt) if dt != BF16 else v
        rt_sc = rtv(NTI * 16).rearrange("p (j e) -> p j e", e=E)
        rt_a = rtv(NTI * 16).rearrange("p (j e) -> p j e", e=E)
        rt_b = rtv(NTI * 16).rearrange("p (j e) -> p j e", e=E)
        rt_c = rtv(NTI * 16).rearrange("p (j e) -> p j e", e=E)
        rt_g = rtv(NTI * 8).rearrange("p (j f) -> p j f", f=4)
        rt_h = rtv(NTI * 8).rearrange("p (j f) -> p j f", f=4)
        rt_s1 = rtv(NTI * 2).rearrange("p (j o) -> p j o", o=1)
        rt_s2 = rtv(NTI * 2).rearrange("p (j o) -> p j o", o=1)
        rt_big = rtv(NTI * 128).rearrange("p (j e f) -> p j e f", e=E, f=E)
        rt_bf = rtv(NTI * E, BF16).rearrange("p (j e) -> p j e", e=E)
        assert _cur[0] <= 8448 - 2048, _cur[0]
        RSCR_F = A5[:, 8448 - 2048:8448].bitcast(F32)
        RSCR = [RSCR_F[:, 0:TT // 1], RSCR_F[:, 512:512 + TT]] if TT <= 512 else None
        RSCR = [RSCR_F[:, 0:TT], RSCR_F[:, 512:512 + TT]]
        XCV = A1[:, 0:QC * TCH].rearrange("p (q t) -> p q t", q=QC)
        ZS = A1[:, QC * TCH:2 * QC * TCH].rearrange("p (q t) -> p q t", q=QC)
        XNC = A5[:, 1152:1152 + NCH_F * TCH].rearrange("p (c t) -> p c t",
                                                       c=NCH_F)
        BREP = A2[:, 0:QC * TCH].rearrange("p (n t) -> p n t", n=QC)
        CREP = A2[:, QC * TCH:2 * QC * TCH].rearrange("p (n t) -> p n t", n=QC)
        XBRS = A5[:, 0:2 * (TCH + 3)].rearrange("p (s t) -> p s t", s=2)
        L1SQ = A2[:, 0:NCH_F * TCH].rearrange("p (c t) -> p c t", c=NCH_F)
        MBUF = A5[:, 1152:1152 + 8 * TCH].rearrange("p (n t) -> p n t", n=8)
        MB2 = A5[:, 1152 + 8 * TCH:1152 + 12 * TCH].rearrange("p (n t) -> p n t", n=4)
        YTOT = A5[:, 1152 + 12 * TCH:1152 + 13 * TCH]
        G1B = A5[:, 1152 + 13 * TCH:1152 + 14 * TCH]
        RSCR1_F = A5[:, 1152 + 8 * TCH:1152 + 12 * TCH].bitcast(F32)
        RSCR1 = [RSCR1_F[:, 0:TCH], RSCR1_F[:, TCH:2 * TCH]]
        UBUF = A4[:, 0:8 * TCH].rearrange("p (n t) -> p n t", n=8)
        HBUF = A4[:, 8 * TCH:16 * TCH].rearrange("p (n t) -> p n t", n=8)
        WOSLOT = [A4[:, 0:4096].rearrange("p (q d) -> p q d", q=QC),
                  A4[:, 4096:8192].rearrange("p (q d) -> p q d", q=QC)]
        DASLOT = [A3[:, 0:8 * TCH].rearrange("p (n t) -> p n t", n=8),
                  A3[:, 4096:4096 + 8 * TCH].rearrange("p (n t) -> p n t", n=8)]
        DTQS = [A6[:, 0:TCH], A6[:, 512:512 + TCH]]
        CVA = A6[:, 0:TCH]
        CVB = A6[:, 512:512 + TCH]
        wq_s = A6[:, 1024:1280].bitcast(BF16).rearrange("p (c k) -> p c k", c=NCH_F)
        wk_s = A6[:, 1280:1536].bitcast(BF16).rearrange("p (c k) -> p c k", c=NCH_F)
        wv_s = A6[:, 1536:1792].bitcast(BF16).rearrange("p (c k) -> p c k", c=NCH_F)
        wo_s = A6[0:KQ, 1792:2304].bitcast(BF16)
        QFM = A4[0:KQ, 0:T]
        KFM = A4[0:KQ, T:2 * T]
        VFM = A4[0:KQ, 2 * T:3 * T]
        KTOK = A4[:, 3 * T:3 * T + NTI * KQ].rearrange("p (c k) -> p c k", k=KQ)
        VTOK = A4[:, 3 * T + NTI * KQ:3 * T + 2 * NTI * KQ].rearrange(
            "p (c k) -> p c k", k=KQ)
        OFM = A5[0:KQ, 0:TT]
        ELU_A = A6[0:KQ, 0:TT]
        ELU_B = A6[KQ:128, 0:TT]
        ELU_C = A6[0:KQ, 512:512 + TT] if TT <= 512 else None


        ydma(RES[:, :, :], x_h[:, :, 0:T])
        for dst, src in [
            (gate0_s[:], g0_h[:]), (gate1_s[:], g1_h[:]),
            (cbias_s[:], cb_h[:]), (negeps_s[:], negeps_h[:]),
            (wx_s[:], wx_h[:]), (wdt_s[:], wdt_h[:]), (convw_s[:], convw_h[:]),
            (convb_s[:], convb_h[:]), (bdt_s[:], bdt_h[:]), (dpar_s[:], dpar_h[:]),
            (alog_s[:], alog_h[:]), (ones_col[:], onescol_h[:]),
            (oned_col[:], onedcol_h[:]), (ones_row[:], onesrow_h[:]),
            (id_bf[:], idbf_h[:]), (causalT[:], causal_h[:]),
            (esel8_s[:], esel8_h[:]), (nsel_s[:], nsel_h[:]),
        ]:
            gdma(dst, src)
        small_wt = WT.n
        s_op(lambda s: s.activation(out=alog_s[:], in_=alog_s[:], func=AF.Exp),
             wait=[(WT, small_wt)])
        Q["s"].pop()
        SS.n -= 1
        aneg_tmp = A6[:, 0:QC * NS].rearrange("p (q n) -> p q n", q=QC)
        s_op(lambda s: s.activation(out=aneg_tmp, in_=alog_s[:], func=AF.Exp),
             wait=[(WT, small_wt)])
        v_op(lambda v: v.tensor_scalar(out=alog_s[:], in0=aneg_tmp, scalar1=-1.0,
                                       scalar2=None, op0=ALU.mult),
             wait=[(SS, SS.n)])
        ANEG = alog_s
        v_op(lambda v: v.tensor_scalar(out=halfb_s[:], in0=bdt_s[:], scalar1=0.5,
                                       scalar2=None, op0=ALU.mult),
             wait=[(WT, small_wt)])
        v_op(lambda v: v.memset(htail_s[:], 0.0))
        v_op(lambda v: v.memset(halo_s[:], 0.0))

        wslot_last_use = [0, 0]
        woslot_last_use = [0, 0]
        cache_ready = {}
        lw_counter = [0]

        def wb_barrier_waits():
            return [(s, s.n) for s in WBSEMS.values() if s.n > 0]

        def load_w_cached(src_fp32, cache_ap, key, dst_flat, gsem, ysem,
                          last_use, extra=(), slot_key=None):
            n = cache_ap.free_size()
            dv = dst_flat[:, 0:n]
            wb = WBSEMS[slot_key]
            wbw = [(wb, wb.n)]
            if not cache_ready.get(key):
                wn = gdma(dv, src_fp32,
                          wait=[(ST, last_use)] + wbw + list(extra), sem=gsem)
                emit("g", lambda g, c=cache_ap, s=dv: g.dma_start(c, s),
                     wait=[(gsem, wn)], inc=[(wb, 16)])
                cache_ready[key] = True
                return gsem, wn
            lw_counter[0] += 1
            if lw_counter[0] % 2 == 0:
                wn = gdma(dv, cache_ap,
                          wait=[(ST, last_use)] + wbw + wb_barrier_waits()
                          + list(extra), sem=gsem)
                return gsem, wn
            wn = ydma(dv, cache_ap,
                      wait=[(ST, last_use)] + wbw + wb_barrier_waits()
                      + list(extra), sem=ysem)
            return ysem, wn

        def emit_ln(toff, tlen, xn_view, sq_view, par0=0):
            ntt = tlen // TT
            entry_sv, entry_ss, entry_st = SV.n, SS.n, ST.n
            pr_a, pr_b = pb[4][0:1, 0:TT], pb[5][0:1, 0:TT]
            tr = {"m": [0] * ntt, "ssq": [0] * ntt, "e1": [0] * ntt,
                  "e2": [0] * ntt, "var": [0] * ntt, "rs32": [0] * ntt,
                  "rsb": [0] * ntt, "mrs": [0] * ntt, "brs": [0] * ntt,
                  "bmrs": [0] * ntt, "t1": [0] * ntt, "xn": [0] * ntt,
                  "c1": [0] * ntt}

            def g(d, j):
                return tr[d][j] if j >= 0 else 0

            def emit_lag(j):
                par = (j + par0) % 2
                xbf = xn_view[:, :, j * TT:(j + 1) * TT]
                slot = sq_view[par]
                tr["brs"][j] = mm_group(pb[6][:, 0:TT],
                                        [(ones_row[:], RSB[par][0:1, 0:TT])],
                                        wait=[(SS, tr["rsb"][j]),
                                              (SV, g("t1", j - 1))])
                tr["bmrs"][j] = mm_group(pb[7][:, 0:TT],
                                         [(ones_row[:], RSB[par][0:1, TT:2 * TT])],
                                         wait=[(SV, tr["mrs"][j]),
                                               (SV, g("xn", j - 1))])
                tr["t1"][j] = v_op(lambda v, xbf=xbf, slot=slot: v.tensor_tensor(
                    out=slot, in0=xbf,
                    in1=pb[6][:, 0:TT].unsqueeze(1).broadcast_to([128, NCH_F, TT]),
                    op=ALU.mult),
                    wait=[(ST, tr["brs"][j]), (ST, tr["ssq"][j])])
                tr["xn"][j] = v_op(lambda v, xbf=xbf, slot=slot: v.tensor_tensor(
                    out=xbf, in0=slot,
                    in1=pb[7][:, 0:TT].unsqueeze(1).broadcast_to([128, NCH_F, TT]),
                    op=ALU.subtract),
                    wait=[(ST, tr["bmrs"][j])])

            for j in range(ntt):
                par = (j + par0) % 2
                sl = slice(toff + j * TT, toff + (j + 1) * TT)
                xbf = xn_view[:, :, j * TT:(j + 1) * TT]
                slot = sq_view[par]
                w0 = ([(ST, entry_st), (SS, entry_ss), (IO, 16)]
                      if j == 0 else [])
                tr["c1"][j] = v_op(lambda v, xbf=xbf, sl=sl: v.tensor_copy(
                    xbf, RES[:, :, sl]), wait=w0 + [(SV, g("xn", j - 2))])
                c2 = s_op(lambda s, slot=slot, xbf=xbf: s.activation(
                    out=slot, in_=xbf, func=AF.Square),
                    wait=[(SV, tr["c1"][j]), (SV, g("xn", j - 2))])
                tr["m"][j] = mm_group(
                    pb[2][0:1, 0:TT],
                    [(oned_col[:], xbf[:, c, :]) for c in range(NCH_F)],
                    wait=([(SV, entry_sv)] if j == 0 else []) +
                    [(SV, tr["c1"][j]), (SS, g("e1", j - 1))])
                tr["ssq"][j] = mm_group(
                    pb[3][0:1, 0:TT],
                    [(oned_col[:], slot[:, c, :]) for c in range(NCH_F)],
                    wait=[(SS, c2), (SS, g("e2", j - 1))])
                tr["e1"][j] = s_op(lambda s, par=par: s.copy(
                    out=R_M[par], in_=pb[2][0:1, 0:TT]),
                    wait=[(ST, tr["m"][j]), (SV, g("mrs", j - 2))])
                tr["e2"][j] = s_op(lambda s: s.copy(out=R_A, in_=pb[3][0:1, 0:TT]),
                                   wait=[(ST, tr["ssq"][j]),
                                         (SV, g("var", j - 1))])
                v1 = v_op(lambda v, par=par: v.tensor_tensor(
                    out=pr_a, in0=R_M[par], in1=R_M[par], op=ALU.mult),
                    wait=[(SS, tr["e1"][j]), (SS, g("rs32", j - 1))])
                tr["var"][j] = v_op(lambda v: v.tensor_tensor(
                    out=pr_b, in0=R_A, in1=pr_a, op=ALU.subtract),
                    wait=[(SS, tr["e2"][j])])
                v1 = v_op(lambda v: v.tensor_scalar(
                    out=pr_a, in0=pr_b, scalar1=EPS_LN, scalar2=None,
                    op0=ALU.add))
                s1 = s_op(lambda s: s.activation(out=pr_b, in_=pr_a, func=AF.Ln),
                          wait=[(SV, v1)])
                tr["rsb"][j] = s_op(lambda s, par=par: s.activation(
                    out=RSB[par][0:1, 0:TT], in_=pr_b, func=AF.Exp, scale=-0.5),
                    wait=[(ST, g("brs", j - 2))])
                tr["rs32"][j] = s_op(lambda s: s.activation(
                    out=pr_a, in_=pr_b, func=AF.Exp, scale=-0.5))
                tr["mrs"][j] = v_op(lambda v, par=par: v.tensor_tensor(
                    out=RSB[par][0:1, TT:2 * TT], in0=R_M[par], in1=pr_a,
                    op=ALU.mult),
                    wait=[(SS, tr["rs32"][j]), (ST, g("bmrs", j - 2))])
                if j >= 1:
                    emit_lag(j - 1)
            emit_lag(ntt - 1)
            return {"xn_ready": tr["xn"][ntt - 1], "xn_tiles": list(tr["xn"]),
                    "pb6_free": tr["t1"][ntt - 1], "pb7_free": tr["xn"][ntt - 1]}

        def emit_routing(layer):
            gate_s = gate0_s if layer == 0 else gate1_s
            for j2 in range(NTI):
                prs = [(XN[:, c, j2 * 128:(j2 + 1) * 128], gate_s[:, c, :])
                       for c in range(NCH_F)]
                lg = mm_group(pb[4][:, j2 * E:(j2 + 1) * E], prs,
                              wait=[(SV, SV.n), (SS, SS.n)] if j2 == 0 else ())
            pb4v = pb[4][:, 0:NTI * E].rearrange("p (j e) -> p j e", e=E)
            if layer == 0:
                vv = v_op(lambda v: v.tensor_reduce(out=rt_s1[:, :, 0], in_=pb4v,
                                                    axis=AX.X, op=ALU.max),
                          wait=[(ST, lg)])
                sg = s_op(lambda s: s.activation(out=rt_s2[:, :, 0], in_=rt_s1[:, :, 0],
                                                 func=AF.Tanh, scale=0.5),
                          wait=[(SV, vv)])
                sgv = v_op(lambda v: v.tensor_scalar(out=rt_c[:, :, 0],
                                                     in0=rt_s2[:, :, 0],
                                                     scalar1=0.5, scalar2=0.5,
                                                     op0=ALU.mult, op1=ALU.add),
                           wait=[(SS, sg)])
                sg = sgv
                vv = v_op(lambda v: v.tensor_tensor(
                    out=rt_a, in0=pb4v,
                    in1=rt_s1.broadcast_to([128, NTI, E]), op=ALU.is_ge))
                pb4_free = [(SV, vv)]
                vfin = v_op(lambda v: v.tensor_tensor(
                    out=rt_bf, in0=rt_a,
                    in1=rt_c[:, :, 0:1].broadcast_to([128, NTI, E]), op=ALU.mult),
                    wait=[(SV, sg)])
            else:
                sg = s_op(lambda s: s.activation(out=rt_b, in_=pb4v,
                                                 func=AF.Tanh, scale=0.5),
                          wait=[(ST, lg)])
                pb4_free = [(SS, sg)]
                sg = v_op(lambda v: v.tensor_scalar(out=rt_sc, in0=rt_b,
                                                    scalar1=0.5, scalar2=0.5,
                                                    op0=ALU.mult, op1=ALU.add),
                          wait=[(SS, sg)])
                vv = v_op(lambda v: v.tensor_tensor(
                    out=rt_a, in0=rt_sc,
                    in1=cbias_s[:].unsqueeze(1).broadcast_to([128, NTI, E]),
                    op=ALU.add), wait=[(SV, sg)])
                rta4 = rt_a.rearrange("p j (g two) -> p j g two", two=2)
                vv = v_op(lambda v: v.tensor_tensor(out=rt_g, in0=rta4[:, :, :, 0],
                                                    in1=rta4[:, :, :, 1], op=ALU.add))
                vv = v_op(lambda v: v.tensor_tensor(out=rt_s1[:, :, 0], in0=rt_g[:, :, 0],
                                                    in1=rt_g[:, :, 1], op=ALU.max))
                vv = v_op(lambda v: v.tensor_tensor(out=rt_s2[:, :, 0], in0=rt_g[:, :, 2],
                                                    in1=rt_g[:, :, 3], op=ALU.max))
                vv = v_op(lambda v: v.tensor_tensor(out=rt_h[:, :, 0], in0=rt_s1[:, :, 0],
                                                    in1=rt_s2[:, :, 0], op=ALU.min))
                vv = v_op(lambda v: v.tensor_tensor(out=rt_s1[:, :, 0], in0=rt_g[:, :, 0],
                                                    in1=rt_g[:, :, 1], op=ALU.min))
                vv = v_op(lambda v: v.tensor_tensor(out=rt_s2[:, :, 0], in0=rt_g[:, :, 2],
                                                    in1=rt_g[:, :, 3], op=ALU.min))
                vv = v_op(lambda v: v.tensor_tensor(out=rt_h[:, :, 1], in0=rt_s1[:, :, 0],
                                                    in1=rt_s2[:, :, 0], op=ALU.max))
                vv = v_op(lambda v: v.tensor_tensor(out=rt_s2[:, :, 0], in0=rt_h[:, :, 0],
                                                    in1=rt_h[:, :, 1], op=ALU.max))
                vv = v_op(lambda v: v.tensor_tensor(
                    out=rt_h, in0=rt_g,
                    in1=rt_s2.broadcast_to([128, NTI, 4]), op=ALU.is_ge))
                rt_b4 = rt_b.rearrange("p j (g two) -> p j g two", two=2)
                rt_c4 = rt_c.rearrange("p j (g two) -> p j g two", two=2)
                emv = rt_h.unsqueeze(3).broadcast_to([128, NTI, 4, 2])
                vv = v_op(lambda v: v.tensor_tensor(out=rt_b4, in0=rta4, in1=emv,
                                                    op=ALU.mult))
                vv = v_op(lambda v: v.tensor_scalar(out=rt_c4, in0=emv, scalar1=1.0,
                                                    scalar2=None, op0=ALU.subtract))
                vv = v_op(lambda v: v.tensor_tensor(
                    out=rt_a, in0=rt_c,
                    in1=negeps_s[:].unsqueeze(1).broadcast_to([128, NTI, E]),
                    op=ALU.mult))
                vv = v_op(lambda v: v.tensor_tensor(out=rt_c, in0=rt_b, in1=rt_a,
                                                    op=ALU.add))
                vv = v_op(lambda v: v.tensor_tensor(
                    out=rt_big,
                    in0=rt_c.unsqueeze(2).broadcast_to([128, NTI, E, E]),
                    in1=rt_c.unsqueeze(3).broadcast_to([128, NTI, E, E]),
                    op=ALU.is_gt))
                vv = v_op(lambda v: v.tensor_reduce(out=rt_a, in_=rt_big,
                                                    axis=AX.X, op=ALU.add))
                vv = v_op(lambda v: v.tensor_scalar(out=rt_b, in0=rt_a, scalar1=3.5,
                                                    scalar2=None, op0=ALU.is_lt))
                vv = v_op(lambda v: v.tensor_tensor(out=rt_a, in0=rt_b, in1=rt_sc,
                                                    op=ALU.mult))
                vv = v_op(lambda v: v.tensor_reduce(out=rt_s1[:, :, 0], in_=rt_a,
                                                    axis=AX.X, op=ALU.add))
                vv = v_op(lambda v: v.tensor_scalar(out=rt_s2[:, :, 0], in0=rt_s1[:, :, 0],
                                                    scalar1=1e-20, scalar2=None,
                                                    op0=ALU.add))
                vv = v_op(lambda v: v.reciprocal(out=rt_s1[:, :, 0], in_=rt_s2[:, :, 0]))
                vv = v_op(lambda v: v.tensor_scalar(out=rt_s2[:, :, 0], in0=rt_s1[:, :, 0],
                                                    scalar1=2.5, scalar2=None,
                                                    op0=ALU.mult))
                vfin = v_op(lambda v: v.tensor_tensor(
                    out=rt_bf, in0=rt_a,
                    in1=rt_s2.broadcast_to([128, NTI, E]), op=ALU.mult))
            for j2 in range(NTI):
                tp = t_op(lambda t, j2=j2: t.transpose(
                    pbf[5][0:E, 0:128], rt_bf[:, j2, :], id_bf[:]),
                    wait=[(SV, vfin), (SS, SS.n)])
                s_op(lambda s, j2=j2: s.copy(
                    out=scale_fm[:, j2 * 128:(j2 + 1) * 128], in_=pbf[5][0:E, 0:128]),
                    wait=[(ST, tp)])
            return SS.n, pb4_free

        def emit_ffn(layer):
            prev_v, prev_s = SV.n, SS.n
            lnr = emit_ln(0, T, XN, SQ2)
            xn_ready = lnr["xn_ready"]
            sc_done, pb4_free = emit_routing(layer)
            wg_h_, wu_h_, wd_h_ = ((wg0_h, wu0_h, wd0_h) if layer == 0
                                   else (wg1_h, wu1_h, wd1_h))
            has_shared = (layer == 1)
            nexp = E + (1 if has_shared else 0)
            mat = [0]

            wc = wc_l0 if layer == 0 else wc_l1

            def load_w(src_ap, key_idx):
                slot = mat[0] % 2
                sem, wn = load_w_cached(src_ap, wc[key_idx], (layer, key_idx),
                                        WSLOT[slot], WSEM[slot], YWSEM[slot],
                                        wslot_last_use[slot],
                                        slot_key=("ws", slot))
                mat[0] += 1
                return slot, sem, wn

            XSB = [XS_E, A4[:, NCH_F * TT:2 * NCH_F * TT].rearrange(
                "p (c t) -> p c t", c=NCH_F)]
            gbank_free = [prev_s, prev_s]
            ubank_free = [xn_ready, xn_ready]
            bc_free = [pb4_free, [(SS, sc_done)]]
            screp_free = [0, 0]
            silub_free = [0, 0]
            down_free_v = [0] * 8
            rscr_free = [prev_s, prev_s]
            st_state = {"pstore_free": 0, "xs_free": [0, 0],
                        "xs_ready": [xn_ready, xn_ready],
                        "screp_ready": [0, 0], "pdone": [0] * nexp}

            def prep_scale(e, tsl):
                par = e % 2
                bcast = mm_group(
                    pb[4 + par][:, 0:TT],
                    [(esel8_s[:, e * 128:(e + 1) * 128], scale_fm[:, tsl])],
                    wait=([(SS, sc_done)] +
                          (list(bc_free[par]) if isinstance(bc_free[par], list)
                           else [(SS, bc_free[par])])))
                scv = s_op(lambda s, par=par: s.copy(out=SCREP[par],
                                                     in_=pb[4 + par][:, 0:TT]),
                           wait=[(ST, bcast), (SS, sc_done),
                                 (SV, screp_free[par])])
                bc_free[par] = scv
                st_state["screp_ready"][par] = scv
                if layer == 0:
                    xsv = v_op(lambda v, par=par, tsl=tsl: v.tensor_tensor(
                        out=XSB[par][:, :, 0:TT], in0=XN[:, :, tsl],
                        in1=SCREP[par].unsqueeze(1).broadcast_to(
                            [128, NCH_F, TT]), op=ALU.mult),
                        wait=[(SS, scv), (SV, xn_ready),
                              (ST, st_state["xs_free"][par])])
                    st_state["xs_ready"][par] = xsv
                    screp_free[par] = xsv

            for j in range(NTT):
                tsl = slice(j * TT, (j + 1) * TT)
                prep_scale(0, tsl)
                for e in range(nexp):
                    par = e % 2
                    shared = (e == E)
                    if not shared:
                        gsrc = wg_h_[e].rearrange("p c i -> p (c i)")
                        usrc = wu_h_[e].rearrange("p c i -> p (c i)")
                    else:
                        gsrc = wgs_h[:].rearrange("p c i -> p (c i)")
                        usrc = wus_h[:].rearrange("p c i -> p (c i)")
                    gslot, gsem, gwn = load_w(gsrc, e * 2)
                    uslot, usem, uwn = load_w(usrc, e * 2 + 1)
                    gv = WSLOT[gslot].rearrange("p (c i) -> p c i", c=NCH_F)
                    uv = WSLOT[uslot].rearrange("p (c i) -> p c i", c=NCH_F)
                    rhs_src = (XSB[par][:, :, 0:TT] if (layer == 0 and not shared)
                               else XN[:, :, tsl])
                    rhs_rdy = (st_state["xs_ready"][par]
                               if (layer == 0 and not shared) else xn_ready)
                    for ic in range(4):
                        gps = pb[ic % 2][:, 0:TT]
                        gdn = mm_group(gps, [(gv[:, c, ic * 128:(ic + 1) * 128],
                                              rhs_src[:, c, :])
                                             for c in range(NCH_F)],
                                       wait=[(gsem, gwn),
                                             (SS, gbank_free[ic % 2]),
                                             (SV, rhs_rdy)], split=True)
                        ups = pb[2 + ic % 2][:, 0:TT]
                        udn = mm_group(ups, [(uv[:, c, ic * 128:(ic + 1) * 128],
                                              rhs_src[:, c, :])
                                             for c in range(NCH_F)],
                                       wait=[(usem, uwn),
                                             (SV, ubank_free[ic % 2])],
                                       split=True)
                        if ic == 0 and e + 1 < nexp and e + 1 < E:
                            prep_scale(e + 1, tsl)
                        sil = s_op(lambda s, ic=ic, gps=gps: s.activation(
                            out=SILUB[ic % 2], in_=gps, func=AF.Silu),
                            wait=[(ST, gdn), (SV, silub_free[ic % 2])])
                        gbank_free[ic % 2] = sil
                        if layer == 0 or shared:
                            dst = (PS_SH[:, ic, 0:TT] if shared
                                   else PSTORE[:, e, ic, 0:TT])
                            pv = v_op(lambda v, ic=ic, ups=ups, dst=dst:
                                      v.tensor_tensor(out=dst, in0=SILUB[ic % 2],
                                                      in1=ups, op=ALU.mult),
                                      wait=[(SS, sil), (ST, udn),
                                            (ST, st_state["pstore_free"])])
                            ubank_free[ic % 2] = pv
                            silub_free[ic % 2] = pv
                        else:
                            pv = v_op(lambda v, ic=ic, ups=ups: v.tensor_tensor(
                                out=PTMP, in0=SILUB[ic % 2], in1=ups,
                                op=ALU.mult),
                                wait=[(SS, sil), (ST, udn),
                                      (SS, st_state["screp_ready"][par])])
                            ubank_free[ic % 2] = pv
                            silub_free[ic % 2] = pv
                            pv = v_op(lambda v, e=e, ic=ic, par=par:
                                      v.tensor_tensor(
                                          out=PSTORE[:, e, ic, 0:TT], in0=PTMP,
                                          in1=SCREP[par], op=ALU.mult),
                                      wait=[(ST, st_state["pstore_free"])])
                            if ic == 3:
                                screp_free[par] = pv
                    st_state["pdone"][e] = SV.n
                    wslot_last_use[gslot] = ST.n
                    wslot_last_use[uslot] = ST.n
                    if layer == 0 and not shared:
                        st_state["xs_free"][par] = ST.n
                dn_bank = [0] * 8
                for e in range(nexp):
                    shared = (e == E)
                    dsrc = (wds_h[:] if shared else wd_h_[e]).rearrange(
                        "p i d -> p (i d)")
                    dslot, dsem, dwn = load_w(dsrc, (16 + e) if layer == 0 else (18 + e))
                    dv = WSLOT[dslot].rearrange("p (i d) -> p i d", i=4)
                    for d in range(8):
                        prs = []
                        for ic in range(4):
                            for hh in range(2):
                                hs = slice(hh * HALF, (hh + 1) * HALF)
                                psrc = (PS_SH[:, ic, hs] if shared
                                        else PSTORE[:, e, ic, hs])
                                prs.append((dv[:, ic, d * 128:(d + 1) * 128],
                                            psrc, hh))

                        def fn(t, prs=prs, d=d, first=(e == 0)):
                            r = None
                            for i, (lh, rh, hh) in enumerate(prs):
                                r = t.matmul(pb[d][:, hh * HALF:(hh + 1) * HALF],
                                             lh, rh, start=(first and i == 0),
                                             stop=False, skip_group_check=True)
                            return r
                        w = [(dsem, dwn), (SV, st_state["pdone"][e])]
                        if e == 0:
                            if d < 2:
                                w.append((SS, gbank_free[d]))
                            elif d < 4:
                                w.append((SV, ubank_free[d - 2]))
                            elif d < 6:
                                bf = bc_free[d - 4]
                                w += (list(bf) if isinstance(bf, list)
                                      else [(SS, bf)])
                            else:
                                w += [(SV, xn_ready)]
                            w.append((SV, down_free_v[d]))
                        emit("t", fn, wait=w, inc=[(ST, 1)])
                        dn_bank[d] = ST.n
                    wslot_last_use[dslot] = ST.n
                for d in range(8):
                    rv = v_op(lambda v, d=d, tsl=tsl: v.scalar_tensor_tensor(
                        out=RSCR[d % 2], in0=pb[d][:, 0:TT], scalar=0.5,
                        in1=RES[:, d, tsl], op0=ALU.mult, op1=ALU.add),
                        wait=[(ST, dn_bank[d]), (SS, rscr_free[d % 2]),
                              (SS, prev_s)])
                    down_free_v[d] = rv
                    cp = s_op(lambda s, d=d, tsl=tsl: s.copy(out=RES[:, d, tsl],
                                                             in_=RSCR[d % 2]),
                              wait=[(SV, rv)])
                    rscr_free[d % 2] = cp
                st_state["pstore_free"] = ST.n

        def emit_l0attn():
            xn_ready = emit_ln(0, T, XN, SQ2)["xn_ready"]
            wna = gdma(wq_s, wq_h[:], wait=[(ST, ST.n), (SV, SV.n), (SS, SS.n)],
                       sem=WSEM[0])
            wnb = gdma(wk_s, wk_h[:], sem=WSEM[1])
            wnc = gdma(wv_s, wv_h[:], sem=WOSEM[0])
            wnd = gdma(wo_s, wo_h[:], sem=WOSEM[1])
            w0waits = [(WSEM[0], wna), (WSEM[1], wnb), (WOSEM[0], wnc),
                       (WOSEM[1], wnd)]
            for j in range(NTT):
                tsl = slice(j * TT, (j + 1) * TT)
                for pi, (wsb, elu) in enumerate([(wq_s, True), (wk_s, True),
                                                 (wv_s, False)]):
                    ps = pb[pi % 2][0:KQ, 0:TT]
                    dn = mm_group(ps, [(wsb[:, c, :], XN[:, c, tsl])
                                       for c in range(NCH_F)],
                                  wait=[(SV, SV.n), (SS, SS.n)] + w0waits,
                                  split=True)
                    if elu:
                        dst = QFM[:, tsl] if pi == 0 else KFM[:, tsl]
                        va = v_op(lambda v, ps=ps: v.tensor_scalar(
                            out=ELU_A, in0=ps, scalar1=0.0, scalar2=None,
                            op0=ALU.max), wait=[(ST, dn)])
                        va = v_op(lambda v, ps=ps: v.tensor_scalar(
                            out=ELU_B[:, 0:TT] if False else ELU_B, in0=ps,
                            scalar1=0.0, scalar2=None, op0=ALU.min))
                        sa = s_op(lambda s: s.activation(out=ELU_C, in_=ELU_B,
                                                         func=AF.Exp),
                                  wait=[(SV, va)])
                        v_op(lambda v, dst=dst: v.tensor_tensor(
                            out=dst, in0=ELU_A, in1=ELU_C, op=ALU.add),
                            wait=[(SS, sa)])
                    else:
                        s_op(lambda s, ps=ps, tsl=tsl: s.copy(out=VFM[:, tsl],
                                                              in_=ps),
                             wait=[(ST, dn)])
            wslot_last_use[0] = wslot_last_use[1] = ST.n
            woslot_last_use[0] = woslot_last_use[1] = ST.n
            for c in range(NTI):
                csl = slice(c * 128, (c + 1) * 128)
                tp = t_op(lambda t, csl=csl: t.transpose(
                    pbf[5][:, 0:KQ], KFM[:, csl], id_bf[0:KQ, 0:KQ]),
                    wait=[(SV, SV.n), (SS, SS.n)])
                s_op(lambda s, c=c: s.copy(out=KTOK[:, c, :], in_=pbf[5][:, 0:KQ]),
                     wait=[(ST, tp)])
                tp = t_op(lambda t, csl=csl: t.transpose(
                    pbf[5][:, KQ:2 * KQ], VFM[:, csl], id_bf[0:KQ, 0:KQ]),
                    wait=[(SS, SS.n)])
                s_op(lambda s, c=c: s.copy(out=VTOK[:, c, :],
                                           in_=pbf[5][:, KQ:2 * KQ]),
                     wait=[(ST, tp)])
            v_op(lambda v: v.memset(sst_s[:, 0, :], 0.0))
            vz = v_op(lambda v: v.memset(zst_s[:, 0, :], 0.0))
            sc = s_op(lambda s: s.copy(out=sbf_s[:], in_=sst_s[:, 0, :]),
                      wait=[(SV, vz)])
            sc = s_op(lambda s: s.copy(out=zbf_s[:], in_=zst_s[:, 0, :]))
            for c in range(NTI):
                par = c % 2
                csl = slice(c * 128, (c + 1) * 128)
                at = mm_group(pb[2][:, 0:128], [(KFM[:, csl], QFM[:, csl])],
                              wait=[(SV, SV.n)])
                va = v_op(lambda v: v.tensor_tensor(out=atm_s[:], in0=pb[2][:, 0:128],
                                                    in1=causalT[:], op=ALU.mult),
                          wait=[(ST, at)])
                nm = mm_group(pb[3][:, 0:KQ],
                              [(atm_s[:], VTOK[:, c, :]), (QFM[:, csl], sbf_s[:])],
                              wait=[(SV, va), (SS, SS.n)])
                nm = mm_group(pb[3][:, KQ:KQ + 1],
                              [(atm_s[:], ones_col[:]), (QFM[:, csl], zbf_s[:])])
                ds = mm_group(pb[6][0:KQ, 0:KQ],
                              [(KTOK[:, c, :], VTOK[:, c, :])])
                ds = mm_group(pb[6][0:KQ, KQ:KQ + 1], [(KTOK[:, c, :], ones_col[:])])
                vs = v_op(lambda v, par=par: v.tensor_tensor(
                    out=sst_s[:, 1 - par, :], in0=sst_s[:, par, :],
                    in1=pb[6][0:KQ, 0:KQ], op=ALU.add), wait=[(ST, ds)])
                vs = v_op(lambda v, par=par: v.tensor_tensor(
                    out=zst_s[:, 1 - par, :], in0=zst_s[:, par, :],
                    in1=pb[6][0:KQ, KQ:KQ + 1], op=ALU.add))
                s_op(lambda s, par=par: s.copy(out=sbf_s[:], in_=sst_s[:, 1 - par, :]),
                     wait=[(SV, vs)])
                s_op(lambda s, par=par: s.copy(out=zbf_s[:], in_=zst_s[:, 1 - par, :]))
                vd = v_op(lambda v: v.tensor_scalar(out=den_s[:], in0=pb[3][:, KQ:KQ + 1],
                                                    scalar1=1e-6, scalar2=None,
                                                    op0=ALU.add), wait=[(ST, nm)])
                vd = v_op(lambda v: v.reciprocal(out=rcp_s[:], in_=den_s[:]))
                vo = v_op(lambda v: v.tensor_scalar(out=otok_s, in0=pb[3][:, 0:KQ],
                                                    scalar1=rcp_s[:], scalar2=None,
                                                    op0=ALU.mult))
                tp = t_op(lambda t: t.transpose(pbf[5][0:KQ, 128:256], otok_s,
                                                id_bf[:]), wait=[(SV, vo)])
                so = s_op(lambda s, c=c: s.copy(
                    out=OFM[:, (c % (TT // 128)) * 128:(c % (TT // 128)) * 128 + 128],
                    in_=pbf[5][0:KQ, 128:256]), wait=[(ST, tp)])
                if (c + 1) % (TT // 128) == 0:
                    j = c // (TT // 128)
                    tsl = slice(j * TT, (j + 1) * TT)
                    for d in range(8):
                        wod = mm_group(pb[7][:, 0:TT],
                                       [(wo_s[:, d * 128:(d + 1) * 128], OFM[:, 0:TT])],
                                       wait=[(SS, so), (SV, SV.n)])
                        rv = v_op(lambda v, d=d, tsl=tsl: v.scalar_tensor_tensor(
                            out=RSCR[d % 2], in0=pb[7][:, 0:TT], scalar=1.0,
                            in1=RES[:, d, tsl], op0=ALU.mult, op1=ALU.add),
                            wait=[(ST, wod), (SS, SS.n)])
                        so2 = s_op(lambda s, d=d, tsl=tsl: s.copy(
                            out=RES[:, d, tsl], in_=RSCR[d % 2]), wait=[(SV, rv)])
                    so = so2

        def emit_l1attn():
            aneg_flat = alog_s[:].rearrange("p q n -> p (q n)")
            a3_flat = A6[:, 1536:1792]
            v_op(lambda v: v.tensor_scalar(out=a3_flat, in0=aneg_flat,
                                           scalar1=0.0505, scalar2=None,
                                           op0=ALU.mult),
                 wait=[(SV, SV.n), (SS, SS.n), (ST, ST.n)])
            v_op(lambda v: v.tensor_scalar(out=aneg_flat, in0=a3_flat,
                                           scalar1=0.0495 / 0.0505,
                                           scalar2=None, op0=ALU.mult))
            prev_wout = None
            wout_rv_free = [0, 0]
            prev_conv_tail = [0, 0]
            wout_scopy = [0]
            for k in range(NCHK):
                koff = k * TCH
                ksl = slice(koff, koff + TCH)
                xn_ready = emit_ln(koff, TCH, XNC,
                                   [L1SQ[:, :, 0:TT], L1SQ[:, :, 0:TT]],
                                   par0=1)["xn_ready"]
                if k == 0:
                    v_op(lambda v: v.memset(XBRS[:, :, 0:3], 0.0))
                if prev_wout is not None:
                    prev_wout()
                    prev_wout = None
                zs_done = [0] * QC
                xz_bank_free = [SS.n, SS.n]

                def xz_load(grp):
                    slot = grp % 2
                    wsem, wn = load_w_cached(
                        win_h[:, :, grp * 512:(grp + 1) * 512], wc_win[grp],
                        ("win", grp), WSLOT[slot], WSEM[slot], YWSEM[slot],
                        wslot_last_use[slot], slot_key=("ws", slot))
                    gv = WSLOT[slot].rearrange("p (c i) -> p c i", c=NCH_F)
                    return slot, wsem, wn, gv

                def xz_mm(gv, wsem, wn, m, mj):
                    ps = pb[m % 2][:, 0:TCH]
                    dn = mm_group(ps, [(gv[:, cc, mj * 128:(mj + 1) * 128],
                                        XNC[:, cc, 0:TCH])
                                       for cc in range(NCH_F)],
                                  wait=[(wsem, wn), (SV, xn_ready),
                                        (SS, xz_bank_free[m % 2]),
                                        (SV, wout_rv_free[m % 2])], split=True)
                    return ps, dn

                conv_done = [0] * QC
                for grp in range(4):
                    slot, wsem, wn, gv = xz_load(grp)
                    for mj in range(4):
                        m = grp * 4 + mj
                        ps, dn = xz_mm(gv, wsem, wn, m, mj)
                        q = m
                        sq_ = q % 2
                        hv = v_op(lambda v, q=q, sq_=sq_: v.tensor_copy(
                            XBRS[:, sq_, 0:3], halo_s[:, q, :]),
                            wait=[(SV, conv_done[q - 2] if q >= 2
                                   else prev_conv_tail[q])])
                        ev = s_op(lambda s, ps=ps, sq_=sq_: s.copy(
                            out=XBRS[:, sq_, 3:3 + TCH], in_=ps),
                            wait=[(ST, dn), (SV, hv)])
                        xz_bank_free[q % 2] = ev
                        cv = v_op(lambda v, q=q, sq_=sq_: v.tensor_scalar(
                            out=CVA, in0=XBRS[:, sq_, 0:TCH],
                            scalar1=convw_s[:, q, 0:1],
                            scalar2=convb_s[:, q:q + 1],
                            op0=ALU.mult, op1=ALU.add), wait=[(SS, ev)])
                        cv = v_op(lambda v, q=q, sq_=sq_: v.scalar_tensor_tensor(
                            out=CVB, in0=XBRS[:, sq_, 1:1 + TCH],
                            scalar=convw_s[:, q, 1:2], in1=CVA,
                            op0=ALU.mult, op1=ALU.add))
                        cv = v_op(lambda v, q=q, sq_=sq_: v.scalar_tensor_tensor(
                            out=CVA, in0=XBRS[:, sq_, 2:2 + TCH],
                            scalar=convw_s[:, q, 2:3], in1=CVB,
                            op0=ALU.mult, op1=ALU.add))
                        cv = v_op(lambda v, q=q, sq_=sq_: v.scalar_tensor_tensor(
                            out=ZS[:, q, :], in0=XBRS[:, sq_, 3:3 + TCH],
                            scalar=convw_s[:, q, 3:4], in1=CVA,
                            op0=ALU.mult, op1=ALU.add))
                        cv = v_op(lambda v, q=q, sq_=sq_: v.tensor_copy(
                            halo_s[:, q, :], XBRS[:, sq_, TCH:TCH + 3]))
                        conv_done[q] = cv
                    wslot_last_use[slot] = ST.n
                prev_conv_tail = [conv_done[QC - 2], conv_done[QC - 1]]
                wout_done_st = ST.n
                cv_done = [0] * QC
                for q in range(QC):
                    cvs = s_op(lambda s, q=q: s.activation(
                        out=XCV[:, q, :], in_=ZS[:, q, :], func=AF.Silu),
                        wait=[(SV, conv_done[q]), (ST, wout_done_st)])
                    cv_done[q] = cvs
                xcv_done = SS.n
                xd = mm_group(pb[2][0:96, 0:TCH],
                              [(wx_s[:, q, :], XCV[:, q, :]) for q in range(QC)],
                              wait=[(SS, xcv_done), (SV, SV.n)], split=True)
                vmz = v_op(lambda v: v.memset(xdbl_bf[96:128, 0:TCH], 0.0),
                           wait=[(ST, xd)])
                xde = s_op(lambda s: s.copy(out=xdbl_bf[0:96, 0:TCH],
                                            in_=pb[2][0:96, 0:TCH]),
                           wait=[(ST, xd)])
                for n in range(NS):
                    bb = mm_group(pb[3 + n % 2][:, 0:TCH],
                                  [(nsel_s[:, n * 128:(n + 1) * 128],
                                    xdbl_bf[:, 0:TCH])],
                                  wait=[(SS, xde), (SV, vmz), (SS, SS.n),
                                        (SV, SV.n)])
                    s_op(lambda s, n=n, bb=bb: s.copy(out=BREP[:, n, :],
                                                      in_=pb[3 + n % 2][:, 0:TCH]),
                         wait=[(ST, bb)])
                for n in range(NS):
                    cc_ = mm_group(pb[3 + n % 2][:, 0:TCH],
                                   [(nsel_s[:, (16 + n) * 128:(17 + n) * 128],
                                     xdbl_bf[:, 0:TCH])],
                                   wait=[(SS, SS.n)])
                    s_op(lambda s, n=n, cc_=cc_: s.copy(out=CREP[:, n, :],
                                                        in_=pb[3 + n % 2][:, 0:TCH]),
                         wait=[(ST, cc_)])
                brep_done = SS.n
                for grp in range(4, 8):
                    slot, wsem, wn, gv = xz_load(grp)
                    for mj in range(4):
                        m = grp * 4 + mj
                        ps, dn = xz_mm(gv, wsem, wn, m, mj)
                        q = m - QC
                        zq = s_op(lambda s, ps=ps, q=q: s.activation(
                            out=ZS[:, q, :], in_=ps, func=AF.Silu),
                            wait=[(ST, dn)])
                        zs_done[q] = zq
                        xz_bank_free[m % 2] = zq
                    wslot_last_use[slot] = ST.n
                a4_free_st = ST.n
                dtbank_free = [SS.n, SS.n]
                daslot_free = [SV.n, SV.n]
                dtq_free = [SV.n, SV.n]
                for q in range(QC):
                    dtp = 5 if q % 2 == 0 else 6
                    dt_mm = mm_group(pb[dtp][:, 0:TCH],
                                     [(wdt_s[:, q * 128:(q + 1) * 128],
                                       xdbl_bf[0:64, 0:TCH])],
                                     wait=[(SS, dtbank_free[q % 2]),
                                           (SS, xde)])
                    sdt = s_op(lambda s, dtp=dtp, q=q: s.activation(
                        out=DTQS[q % 2], in_=pb[dtp][:, 0:TCH], func=AF.Tanh,
                        scale=0.5, bias=halfb_s[:, q:q + 1]),
                        wait=[(ST, dt_mm), (SV, dtq_free[q % 2])])
                    dtbank_free[q % 2] = sdt
                    vdt = v_op(lambda v, q=q: v.tensor_scalar(
                        out=dtbf_s, in0=DTQS[q % 2], scalar1=0.0495,
                        scalar2=0.0505, op0=ALU.mult, op1=ALU.add),
                        wait=[(SS, sdt)])
                    dtq_free[q % 2] = vdt
                    vdx = v_op(lambda v, q=q: v.tensor_tensor(
                        out=dtx_s, in0=dtbf_s, in1=XCV[:, q, :],
                        op=ALU.mult), wait=[(SS, cv_done[q])])
                    for g8 in range(2):
                        slot = (q * 2 + g8) % 2
                        nsl = slice(g8 * 8, (g8 + 1) * 8)
                        for nn in range(8):
                            w = ([(SV, daslot_free[slot]),
                                  (WBSEMS[("ws", slot)], WBSEMS[("ws", slot)].n)]
                                 if nn == 0 else ())
                            s_op(lambda s, slot=slot, nn=nn, q=q, g8=g8:
                                 s.activation(
                                     out=DASLOT[slot][:, nn, :],
                                     in_=DTQS[q % 2], func=AF.Exp,
                                     scale=A2P[:, q, g8 * 8 + nn:g8 * 8 + nn + 1],
                                     bias=A3P[:, q, g8 * 8 + nn:g8 * 8 + nn + 1]),
                                 wait=w)
                        da_done = SS.n
                        uw = [(SV, vdx), (SS, brep_done)]
                        if q == 0 and g8 == 0:
                            uw.append((ST, a4_free_st))
                        uv = v_op(lambda v, nsl=nsl: v.tensor_tensor(
                            out=UBUF, in0=dtx_s.unsqueeze(1).broadcast_to(
                                [128, 8, TCH]),
                            in1=BREP[:, nsl, :], op=ALU.mult),
                            wait=uw)
                        v_op(lambda v, slot=slot: v.tensor_copy(
                            fixa[:, 0:8], DASLOT[slot][:, :, 0]),
                            wait=[(SS, da_done)])
                        v_op(lambda v, q=q, nsl=nsl: v.tensor_tensor(
                            out=fixb[:, 0:8], in0=fixa[:, 0:8],
                            in1=htail_s[:, q, nsl], op=ALU.mult))
                        v_op(lambda v: v.tensor_tensor(
                            out=fixc[:, 0:8], in0=fixb[:, 0:8],
                            in1=UBUF[:, :, 0], op=ALU.add))
                        v_op(lambda v: v.tensor_copy(UBUF[:, :, 0], fixc[:, 0:8]))
                        v_op(lambda v, slot=slot: v.memset(DASLOT[slot][:, :, 0],
                                                           0.0))
                        scn = v_op(lambda v, slot=slot: v.tensor_tensor_scan(
                            out=HBUF.rearrange("p n t -> p (n t)"),
                            data0=DASLOT[slot].rearrange("p n t -> p (n t)"),
                            data1=UBUF.rearrange("p n t -> p (n t)"),
                            initial=0.0, op0=ALU.mult, op1=ALU.add))
                        daslot_free[slot] = scn
                        v_op(lambda v, q=q, nsl=nsl: v.tensor_copy(
                            htail_s[:, q, nsl], HBUF[:, :, TCH - 1]))
                        v_op(lambda v, nsl=nsl: v.tensor_tensor(
                            out=MBUF, in0=HBUF, in1=CREP[:, nsl, :], op=ALU.mult))
                        v_op(lambda v: v.tensor_tensor(
                            out=MB2, in0=MBUF[:, 0:4, :], in1=MBUF[:, 4:8, :],
                            op=ALU.add),
                            wait=([(SS, wout_scopy[0])]
                                  if (q == 0 and g8 == 0) else ()))
                        v_op(lambda v: v.tensor_tensor(
                            out=MBUF[:, 0:2, :], in0=MB2[:, 0:2, :],
                            in1=MB2[:, 2:4, :], op=ALU.add))
                        dst = YTOT if g8 == 0 else G1B
                        v_op(lambda v, dst=dst: v.tensor_tensor(
                            out=dst, in0=MBUF[:, 0, :], in1=MBUF[:, 1, :],
                            op=ALU.add))
                    v_op(lambda v: v.tensor_tensor(out=MB2[:, 0, :], in0=YTOT,
                                                   in1=G1B, op=ALU.add))
                    v_op(lambda v, q=q: v.scalar_tensor_tensor(
                        out=YTOT, in0=XCV[:, q, :], scalar=dpar_s[:, q:q + 1],
                        in1=MB2[:, 0, :], op0=ALU.mult, op1=ALU.add))
                    v_op(lambda v, q=q: v.tensor_tensor(
                        out=XCV[:, q, :], in0=YTOT, in1=ZS[:, q, :], op=ALU.mult),
                        wait=[(SS, zs_done[q])])
                yg_done = SV.n

                def make_wout(k=k, ksl=ksl, yg_done=yg_done):
                    def emit_wout():
                        for grp in range(4):
                            slot = grp % 2
                            wsem2, wn = load_w_cached(
                                wout_h[:, :, grp * 256:(grp + 1) * 256],
                                wc_wout[grp], ("wout", grp),
                                WOSLOT[slot].rearrange("p q d -> p (q d)"),
                                WOSEM[slot], YWSEM[slot], woslot_last_use[slot],
                                extra=[(SV, yg_done)], slot_key=("wo", slot))
                            for dl in range(2):
                                d = grp * 2 + dl
                                wod = mm_group(
                                    pb[dl % 2][:, 0:TCH],
                                    [(WOSLOT[slot][:, q, dl * 128:(dl + 1) * 128],
                                      XCV[:, q, :]) for q in range(QC)],
                                    wait=[(wsem2, wn), (SV, yg_done),
                                          (SS, SS.n)], split=True)
                                rv = v_op(lambda v, d=d, dl=dl, ksl=ksl:
                                          v.scalar_tensor_tensor(
                                              out=RSCR1[dl % 2],
                                              in0=pb[dl % 2][:, 0:TCH],
                                              scalar=1.0, in1=RES[:, d, ksl],
                                              op0=ALU.mult, op1=ALU.add),
                                          wait=[(ST, wod), (SS, SS.n)])
                                wout_rv_free[dl % 2] = rv
                                s_op(lambda s, d=d, dl=dl, ksl=ksl: s.copy(
                                    out=RES[:, d, ksl], in_=RSCR1[dl % 2]),
                                    wait=[(SV, rv)])
                            woslot_last_use[slot] = ST.n
                        wout_scopy[0] = SS.n
                    return emit_wout
                prev_wout = make_wout()
            prev_wout()

        for ph in phases:
            if ph in ("l0f1", "l0f2"):
                emit_ffn(0)
            elif ph in ("l1f1", "l1f2"):
                emit_ffn(1)
            elif ph == "l0a":
                emit_l0attn()
            elif ph == "l1a":
                emit_l1attn()

        fin = ydma(out_h[:, :, 0:T], RES[:, :, :],
                   wait=[(SS, SS.n), (SV, SV.n)])
        emit("y", lambda y: None, wait=[(IO, fin)])

        block = ctx.enter_context(nc.Block())

        @block.tensor
        def _(t):
            for f in Q["t"]:
                f(t)

        @block.vector
        def _(v):
            for f in Q["v"]:
                f(v)

        @block.scalar
        def _(s):
            for f in Q["s"]:
                f(s)

        @block.gpsimd
        def _(g):
            for f in Q["g"]:
                f(g)

        @block.sync
        def _(y):
            for f in Q["y"]:
                f(y)

    return nc



def _lay_feat_major(w_t):
    K, M = w_t.shape
    return np.ascontiguousarray(w_t.reshape(K // 128, 128, M).transpose(1, 0, 2))


def _prep_inputs(inputs, T=2048):
    f = np.float32
    p = {}
    x = np.asarray(inputs["L0_attn_x"], f)
    xs = []
    for i in range(8):
        xi = np.ascontiguousarray(x[i].T)
        xs.append(_lay_feat_major(xi)[:, :, 0:T])
    p["x"] = xs
    p["l0a_wq"] = _lay_feat_major(np.asarray(inputs["L0_attn_Wq"], f).T)
    p["l0a_wk"] = _lay_feat_major(np.asarray(inputs["L0_attn_Wk"], f).T)
    p["l0a_wv"] = _lay_feat_major(np.asarray(inputs["L0_attn_Wv"], f).T)
    p["l0a_wo"] = np.ascontiguousarray(np.asarray(inputs["L0_attn_Wo"], f).T)
    p["l0f_gate"] = _lay_feat_major(np.asarray(inputs["L0_ffn_gate_w"], f).T)
    p["l0f_wg"] = np.stack([_lay_feat_major(np.asarray(inputs["L0_ffn_w_gate"], f)[e].T)
                            for e in range(8)])
    p["l0f_wu"] = np.stack([_lay_feat_major(np.asarray(inputs["L0_ffn_w_up"], f)[e].T)
                            for e in range(8)])
    p["l0f_wd"] = np.stack([_lay_feat_major(np.asarray(inputs["L0_ffn_w_down"], f)[e].T)
                            for e in range(8)])
    p["l1a_win"] = _lay_feat_major(np.asarray(inputs["L1_attn_W_in"], f))
    cw = np.asarray(inputs["L1_attn_conv_w"], f).reshape(2048, 4)
    p["l1a_convw"] = np.ascontiguousarray(cw.reshape(16, 128, 4).transpose(1, 0, 2))
    p["l1a_convb"] = np.ascontiguousarray(
        np.asarray(inputs["L1_attn_conv_b"], f).reshape(16, 128).T)
    p["l1a_wx"] = _lay_feat_major(np.asarray(inputs["L1_attn_W_x"], f))
    p["l1a_wdt"] = np.ascontiguousarray(np.asarray(inputs["L1_attn_W_dt"], f))
    p["l1a_bdt"] = np.ascontiguousarray(
        np.asarray(inputs["L1_attn_b_dt"], f).reshape(16, 128).T)
    p["l1a_alog"] = np.ascontiguousarray(
        np.asarray(inputs["L1_attn_A_log"], f).reshape(16, 128, 16).transpose(1, 0, 2))
    p["l1a_dpar"] = np.ascontiguousarray(
        np.asarray(inputs["L1_attn_D_param"], f).reshape(16, 128).T)
    p["l1a_wout"] = _lay_feat_major(np.asarray(inputs["L1_attn_W_out"], f))
    p["l1f_gate"] = _lay_feat_major(np.asarray(inputs["L1_ffn_gate_w"], f).T)
    p["l1f_cbias"] = np.ascontiguousarray(np.broadcast_to(
        np.asarray(inputs["L1_ffn_correction_bias"], f), (128, 8)))
    p["l1f_wg"] = np.stack([_lay_feat_major(np.asarray(inputs["L1_ffn_Wg"], f)[e])
                            for e in range(8)])
    p["l1f_wu"] = np.stack([_lay_feat_major(np.asarray(inputs["L1_ffn_Wu"], f)[e])
                            for e in range(8)])
    p["l1f_wd"] = np.stack([_lay_feat_major(np.asarray(inputs["L1_ffn_Wd"], f)[e])
                            for e in range(8)])
    p["l1f_wgs"] = _lay_feat_major(np.asarray(inputs["L1_ffn_Wg_s"], f))
    p["l1f_wus"] = _lay_feat_major(np.asarray(inputs["L1_ffn_Wu_s"], f))
    p["l1f_wds"] = _lay_feat_major(np.asarray(inputs["L1_ffn_Wd_s"], f))
    bf = ml_dtypes.bfloat16
    p["c_onescol"] = np.ones((128, 1), bf)
    p["c_onedcol"] = np.full((128, 1), 1.0 / 1024.0, bf)
    p["c_onesrow"] = np.ones((1, 128), bf)
    p["c_idbf"] = np.eye(128, dtype=bf)
    p["c_causalT"] = np.triu(np.ones((128, 128), bf))
    p["c_negeps"] = np.ascontiguousarray(np.broadcast_to(
        (np.arange(8) * 1e-12).astype(f), (128, 8)))
    esel = np.zeros((8, 8 * 128), bf)
    for e in range(8):
        esel[e, e * 128:(e + 1) * 128] = 1.0
    p["c_esel8"] = esel
    nsel = np.zeros((128, 32 * 128), bf)
    for j in range(32):
        nsel[64 + j, j * 128:(j + 1) * 128] = 1.0
    p["c_nsel"] = nsel
    return p


def kernel(**inputs):
    T = 2048
    p = _prep_inputs(inputs, T)
    nc = build_program(T=T)
    shared = {k: v for k, v in p.items() if k != "x"}
    in_maps = [dict(shared, x=p["x"][i]) for i in range(8)]
    res = run_bass_kernel_spmd(nc, in_maps, core_ids=list(range(8)))
    outs = []
    for i in range(8):
        o = np.asarray(res.results[i]["out"], np.float32)
        outs.append(o.transpose(1, 0, 2).reshape(1024, T).T)
    return np.stack(outs).astype(np.float32)


# version 09
# speedup vs baseline: 1.0510x; 1.0510x over previous
import numpy as np
import ml_dtypes
from contextlib import ExitStack

import concourse.bass as bass
import concourse.mybir as mybir
from concourse.bass_utils import run_bass_kernel_spmd

F32 = mybir.dt.float32
BF16 = mybir.dt.bfloat16
AF = mybir.ActivationFunctionType
ALU = mybir.AluOpType
AX = mybir.AxisListType

D = 1024
NCH_F = 8
E = 8
KQ = 64
DIN = 2048
QC = 16
NS = 16
EPS_LN = 1e-5
ALL_PHASES = ("l0f1", "l0a", "l0f2", "l1f1", "l1a", "l1f2")


class Sem:
    def __init__(self, h):
        self.h = h
        self.n = 0


def build_program(T=2048, phases=ALL_PHASES):
    TT = min(512, T)
    NTT = T // TT
    TCH = TT
    NCHK = NTT
    HALF = TT // 2
    NTI = T // 128

    nc = bass.Bass("TRN2", target_bir_lowering=False)

    def din(name, shape, dt=F32):
        return nc.dram_tensor(name, list(shape), dt, kind="ExternalInput").ap()

    x_h = din("x", [128, NCH_F, T])
    out_h = nc.dram_tensor("out", [128, NCH_F, T], F32, kind="ExternalOutput").ap()

    wq_h = din("l0a_wq", [128, NCH_F, KQ])
    wk_h = din("l0a_wk", [128, NCH_F, KQ])
    wv_h = din("l0a_wv", [128, NCH_F, KQ])
    wo_h = din("l0a_wo", [KQ, D])
    g0_h = din("l0f_gate", [128, NCH_F, E])
    wg0_h = din("l0f_wg", [E, 128, NCH_F, 512])
    wu0_h = din("l0f_wu", [E, 128, NCH_F, 512])
    wd0_h = din("l0f_wd", [E, 128, 4, D])
    win_h = din("l1a_win", [128, NCH_F, 2 * DIN])
    convw_h = din("l1a_convw", [128, QC, 4])
    convb_h = din("l1a_convb", [128, QC])
    wx_h = din("l1a_wx", [128, QC, 96])
    wdt_h = din("l1a_wdt", [64, DIN])
    bdt_h = din("l1a_bdt", [128, QC])
    alog_h = din("l1a_alog", [128, QC, NS])
    dpar_h = din("l1a_dpar", [128, QC])
    wout_h = din("l1a_wout", [128, QC, D])
    g1_h = din("l1f_gate", [128, NCH_F, E])
    cb_h = din("l1f_cbias", [128, E])
    wg1_h = din("l1f_wg", [E, 128, NCH_F, 512])
    wu1_h = din("l1f_wu", [E, 128, NCH_F, 512])
    wd1_h = din("l1f_wd", [E, 128, 4, D])
    wgs_h = din("l1f_wgs", [128, NCH_F, 512])
    wus_h = din("l1f_wus", [128, NCH_F, 512])
    wds_h = din("l1f_wds", [128, 4, D])
    onescol_h = din("c_onescol", [128, 1], BF16)
    onedcol_h = din("c_onedcol", [128, 1], BF16)
    onesrow_h = din("c_onesrow", [1, 128], BF16)
    idbf_h = din("c_idbf", [128, 128], BF16)
    causal_h = din("c_causalT", [128, 128], BF16)
    negeps_h = din("c_negeps", [128, E], F32)
    esel8_h = din("c_esel8", [E, E * 128], BF16)
    nsel_h = din("c_nsel", [128, 32 * 128], BF16)

    wc_l0 = nc.dram_tensor("wc_l0", [24, 128, 4096], BF16, kind="Internal").ap()
    wc_l1 = nc.dram_tensor("wc_l1", [27, 128, 4096], BF16, kind="Internal").ap()
    wc_win = nc.dram_tensor("wc_win", [8, 128, 4096], BF16, kind="Internal").ap()
    wc_wout = nc.dram_tensor("wc_wout", [4, 128, 4096], BF16, kind="Internal").ap()

    with ExitStack() as ctx:
        def sbuf(name, shape, dt):
            return ctx.enter_context(nc.sbuf_tensor(name, list(shape), dt))

        RES = sbuf("RES", [128, NCH_F, T], F32)
        A1 = sbuf("A1", [128, 16384], BF16)
        A2 = sbuf("A2", [128, 16384], BF16)
        A3 = sbuf("A3", [128, 8192], BF16)
        A4 = sbuf("A4", [128, 8192], BF16)
        A5 = sbuf("A5", [128, 8448], BF16)
        A6 = sbuf("A6", [128, 2560], F32)

        gate0_s = sbuf("gate0_s", [128, NCH_F, E], BF16)
        gate1_s = sbuf("gate1_s", [128, NCH_F, E], BF16)
        cbias_s = sbuf("cbias_s", [128, E], F32)
        negeps_s = sbuf("negeps_s", [128, E], F32)
        wx_s = sbuf("wx_s", [128, QC, 96], BF16)
        wdt_s = sbuf("wdt_s", [64, DIN], BF16)
        convw_s = sbuf("convw_s", [128, QC, 4], F32)
        convb_s = sbuf("convb_s", [128, QC], F32)
        bdt_s = sbuf("bdt_s", [128, QC], F32)
        dpar_s = sbuf("dpar_s", [128, QC], F32)
        halfb_s = sbuf("halfb_s", [128, QC], F32)
        alog_s = sbuf("alog_s", [128, QC, NS], F32)
        htail_s = sbuf("htail_s", [128, QC, NS], BF16)
        halo_s = sbuf("halo_s", [128, QC, 3], BF16)
        ones_col = sbuf("ones_col", [128, 1], BF16)
        oned_col = sbuf("oned_col", [128, 1], BF16)
        ones_row = sbuf("ones_row", [1, 128], BF16)
        id_bf = sbuf("id_bf", [128, 128], BF16)
        causalT = sbuf("causalT", [128, 128], BF16)
        esel8_s = sbuf("esel8_s", [E, E * 128], BF16)
        nsel_s = sbuf("nsel_s", [128, 32 * 128], BF16)
        R_M = [A6[0:1, 0:TT], A6[0:1, TT:2 * TT]]
        R_A = A6[0:1, 2 * TT:3 * TT]
        RSB = [A6[0:1, 3 * TT:4 * TT].bitcast(BF16),
               A6[0:1, 4 * TT:5 * TT].bitcast(BF16)]
        scale_fm = A6[0:E, 2 * TT:2 * TT + T // 2].bitcast(BF16)
        xdbl_bf = A6[:, 2048:2304].bitcast(BF16)
        dtbf_s = A6[:, 1024:1280].bitcast(BF16)[:, 0:TCH]
        dtx_s = A6[:, 1280:1536].bitcast(BF16)[:, 0:TCH]
        A3P = A6[:, 1536:1792].rearrange("p (q n) -> p q n", q=QC)
        A2P = alog_s
        fixa = sbuf("fixa", [128, NS], F32)
        fixb = sbuf("fixb", [128, NS], F32)
        fixc = sbuf("fixc", [128, NS], F32)
        sst_s = sbuf("sst_s", [KQ, 2, KQ], F32)
        sbf_s = sbuf("sbf_s", [KQ, KQ], BF16)
        zst_s = sbuf("zst_s", [KQ, 2, 1], F32)
        zbf_s = sbuf("zbf_s", [KQ, 1], BF16)
        atm_s = sbuf("atm_s", [128, 128], BF16)
        otok_s = A6[:, 2304:2336].bitcast(BF16)
        den_s = A6[:, 2336:2337]
        rcp_s = A6[:, 2337:2338]

        pb = [ctx.enter_context(nc.psum_tensor(f"pb{i}", [128, 512], F32))
              for i in range(8)]
        pbf = [p[:].bitcast(BF16) for p in pb]

        WT = Sem(ctx.enter_context(nc.semaphore("wt")))
        WS0 = Sem(ctx.enter_context(nc.semaphore("ws0")))
        WS1 = Sem(ctx.enter_context(nc.semaphore("ws1")))
        WO0 = Sem(ctx.enter_context(nc.semaphore("wo0")))
        WO1 = Sem(ctx.enter_context(nc.semaphore("wo1")))
        WSEM = [WS0, WS1]
        WOSEM = [WO0, WO1]
        YW0 = Sem(ctx.enter_context(nc.semaphore("yw0")))
        YW1 = Sem(ctx.enter_context(nc.semaphore("yw1")))
        YW2 = Sem(ctx.enter_context(nc.semaphore("yw2")))
        WS2 = Sem(ctx.enter_context(nc.semaphore("ws2")))
        YWSEM = [YW0, YW1, YW2]
        WBSEMS = {k: Sem(ctx.enter_context(nc.semaphore(f"wb{i}")))
                  for i, k in enumerate([("ws", 0), ("ws", 1),
                                         ("wo", 0), ("wo", 1), ("ws", 2)])}
        IO = Sem(ctx.enter_context(nc.semaphore("io")))
        ST = Sem(ctx.enter_context(nc.semaphore("st")))
        SV = Sem(ctx.enter_context(nc.semaphore("sv")))
        SS = Sem(ctx.enter_context(nc.semaphore("ss")))

        Q = {e: [] for e in ("t", "v", "s", "g", "y")}

        SELF_SEM = {"v": SV, "s": SS}

        def emit(eng, fn, wait=(), inc=()):
            for s, v in wait:
                assert v <= s.n, f"wait on future value {v} > {s.n} ({eng})"
            wait = list(wait)
            own = SELF_SEM.get(eng)
            if own is not None and own.n > 0:
                wait.append((own, own.n))
            waits = {}
            for s, v in wait:
                if v > 0:
                    waits[s.h] = max(waits.get(s.h, 0), int(v))
            waits = list(waits.items())
            incs = []
            for s, dn in inc:
                s.n += dn
                incs.append((s.h, dn))

            def run(e, fn=fn, waits=waits, incs=incs):
                for h, v in waits:
                    e.wait_ge(h, v)
                r = fn(e)
                for h, dn in incs:
                    r.then_inc(h, dn)

            Q[eng].append(run)

        def t_op(fn, wait=()):
            emit("t", fn, wait=wait, inc=[(ST, 1)])
            return ST.n

        def v_op(fn, wait=()):
            emit("v", fn, wait=wait, inc=[(SV, 1)])
            return SV.n

        def s_op(fn, wait=()):
            emit("s", fn, wait=wait, inc=[(SS, 1)])
            return SS.n

        def gdma(out_ap, in_ap, wait=(), sem=None):
            sem = WT if sem is None else sem
            emit("g", lambda g, o=out_ap, i=in_ap: g.dma_start(o, i),
                 wait=wait, inc=[(sem, 16)])
            return sem.n

        def ydma(out_ap, in_ap, wait=(), sem=None):
            sem = IO if sem is None else sem
            emit("y", lambda y, o=out_ap, i=in_ap: y.dma_start(o, i),
                 wait=wait, inc=[(sem, 16)])
            return sem.n

        def mm_group(ps_ap, pairs, wait=(), start=True, stop=True, split=False):
            def fn(t, ps=ps_ap, prs=list(pairs), st0=start, sp0=stop, sp=split):
                r = None
                n = len(prs)
                if not sp:
                    for i, (lh, rh) in enumerate(prs):
                        r = t.matmul(ps, lh, rh, start=(st0 and i == 0),
                                     stop=(sp0 and i == n - 1),
                                     skip_group_check=True)
                    return r
                H = prs[0][1].shape[-1] // 2
                for i, (lh, rh) in enumerate(prs):
                    r = t.matmul(ps[:, 0:H], lh, rh[:, 0:H],
                                 start=(st0 and i == 0), stop=False,
                                 skip_group_check=True)
                    r = t.matmul(ps[:, H:2 * H], lh, rh[:, H:2 * H],
                                 start=False,
                                 stop=(sp0 and i == n - 1),
                                 skip_group_check=True)
                return r
            emit("t", fn, wait=wait, inc=[(ST, 1)])
            return ST.n

        XN = A2[:, 0:NCH_F * T].rearrange("p (c t) -> p c t", c=NCH_F)
        SQ2 = [A1[:, 0:NCH_F * TT].rearrange("p (c t) -> p c t", c=NCH_F),
               A1[:, NCH_F * TT:2 * NCH_F * TT].rearrange("p (c t) -> p c t",
                                                          c=NCH_F)]
        PSTORE = A1[:, 0:E * 4 * TT].rearrange("p (e i t) -> p e i t", e=E, i=4)
        XS_E = A4[:, 0:NCH_F * TT].rearrange("p (c t) -> p c t", c=NCH_F)
        PS_SH = A4[:, 4096:4096 + 4 * TT].rearrange("p (i t) -> p i t", i=4)
        WSLOT = [A3[:, 0:4096], A3[:, 4096:8192]]
        SCREP = [A5[:, 0:TT], A5[:, TT:2 * TT]]
        SILUB = [A5[:, 2 * TT:3 * TT], A5[:, 3 * TT:4 * TT]]
        PTMP = A5[:, 4 * TT:5 * TT]
        _cur = [5 * TT]
        def rtv(sz_bf, dt=F32):
            off = _cur[0]
            _cur[0] += sz_bf
            v = A5[:, off:off + sz_bf]
            return v.bitcast(dt) if dt != BF16 else v
        rt_sc = rtv(NTI * 16).rearrange("p (j e) -> p j e", e=E)
        rt_a = rtv(NTI * 16).rearrange("p (j e) -> p j e", e=E)
        rt_b = rtv(NTI * 16).rearrange("p (j e) -> p j e", e=E)
        rt_c = rtv(NTI * 16).rearrange("p (j e) -> p j e", e=E)
        rt_g = rtv(NTI * 8).rearrange("p (j f) -> p j f", f=4)
        rt_h = rtv(NTI * 8).rearrange("p (j f) -> p j f", f=4)
        rt_s1 = rtv(NTI * 2).rearrange("p (j o) -> p j o", o=1)
        rt_s2 = rtv(NTI * 2).rearrange("p (j o) -> p j o", o=1)
        rt_big = rtv(NTI * 128).rearrange("p (j e f) -> p j e f", e=E, f=E)
        rt_bf = rtv(NTI * E, BF16).rearrange("p (j e) -> p j e", e=E)
        assert _cur[0] <= 8448 - 2048, _cur[0]
        RSCR_F = A5[:, 8448 - 2048:8448].bitcast(F32)
        RSCR = [RSCR_F[:, 0:TT // 1], RSCR_F[:, 512:512 + TT]] if TT <= 512 else None
        RSCR = [RSCR_F[:, 0:TT], RSCR_F[:, 512:512 + TT]]
        XCV = A1[:, 0:QC * TCH].rearrange("p (q t) -> p q t", q=QC)
        ZS = A1[:, QC * TCH:2 * QC * TCH].rearrange("p (q t) -> p q t", q=QC)
        XNC = A5[:, 1152:1152 + NCH_F * TCH].rearrange("p (c t) -> p c t",
                                                       c=NCH_F)
        BREP = A2[:, 0:QC * TCH].rearrange("p (n t) -> p n t", n=QC)
        CREP = A2[:, QC * TCH:2 * QC * TCH].rearrange("p (n t) -> p n t", n=QC)
        XBRS = A5[:, 0:2 * (TCH + 3)].rearrange("p (s t) -> p s t", s=2)
        L1SQ = A2[:, 0:NCH_F * TCH].rearrange("p (c t) -> p c t", c=NCH_F)
        MBUF = A5[:, 1152:1152 + 8 * TCH].rearrange("p (n t) -> p n t", n=8)
        MB2 = A5[:, 1152 + 8 * TCH:1152 + 12 * TCH].rearrange("p (n t) -> p n t", n=4)
        YTOT = A5[:, 1152 + 12 * TCH:1152 + 13 * TCH]
        G1B = A5[:, 1152 + 13 * TCH:1152 + 14 * TCH]
        RSCR1_F = A5[:, 1152 + 8 * TCH:1152 + 12 * TCH].bitcast(F32)
        RSCR1 = [RSCR1_F[:, 0:TCH], RSCR1_F[:, TCH:2 * TCH]]
        UBUF = A4[:, 0:8 * TCH].rearrange("p (n t) -> p n t", n=8)
        HBUF = A4[:, 8 * TCH:16 * TCH].rearrange("p (n t) -> p n t", n=8)
        WOSLOT = [A4[:, 0:4096].rearrange("p (q d) -> p q d", q=QC),
                  A4[:, 4096:8192].rearrange("p (q d) -> p q d", q=QC)]
        DASLOT = [A3[:, 0:8 * TCH].rearrange("p (n t) -> p n t", n=8),
                  A3[:, 4096:4096 + 8 * TCH].rearrange("p (n t) -> p n t", n=8)]
        DTQS = [A6[:, 0:TCH], A6[:, 512:512 + TCH]]
        CVA = A6[:, 0:TCH]
        CVB = A6[:, 512:512 + TCH]
        wq_s = A6[:, 1024:1280].bitcast(BF16).rearrange("p (c k) -> p c k", c=NCH_F)
        wk_s = A6[:, 1280:1536].bitcast(BF16).rearrange("p (c k) -> p c k", c=NCH_F)
        wv_s = A6[:, 1536:1792].bitcast(BF16).rearrange("p (c k) -> p c k", c=NCH_F)
        wo_s = A6[0:KQ, 1792:2304].bitcast(BF16)
        QFM = A4[0:KQ, 0:T]
        KFM = A4[0:KQ, T:2 * T]
        VFM = A4[0:KQ, 2 * T:3 * T]
        KTOK = A4[:, 3 * T:3 * T + NTI * KQ].rearrange("p (c k) -> p c k", k=KQ)
        VTOK = A4[:, 3 * T + NTI * KQ:3 * T + 2 * NTI * KQ].rearrange(
            "p (c k) -> p c k", k=KQ)
        OFM = A5[0:KQ, 0:TT]
        ELU_A = A6[0:KQ, 0:TT]
        ELU_B = A6[KQ:128, 0:TT]
        ELU_C = A6[0:KQ, 512:512 + TT] if TT <= 512 else None


        ydma(RES[:, :, :], x_h[:, :, 0:T])
        for dst, src in [
            (gate0_s[:], g0_h[:]), (gate1_s[:], g1_h[:]),
            (cbias_s[:], cb_h[:]), (negeps_s[:], negeps_h[:]),
            (wx_s[:], wx_h[:]), (wdt_s[:], wdt_h[:]), (convw_s[:], convw_h[:]),
            (convb_s[:], convb_h[:]), (bdt_s[:], bdt_h[:]), (dpar_s[:], dpar_h[:]),
            (alog_s[:], alog_h[:]), (ones_col[:], onescol_h[:]),
            (oned_col[:], onedcol_h[:]), (ones_row[:], onesrow_h[:]),
            (id_bf[:], idbf_h[:]), (causalT[:], causal_h[:]),
            (esel8_s[:], esel8_h[:]), (nsel_s[:], nsel_h[:]),
        ]:
            gdma(dst, src)
        small_wt = WT.n
        s_op(lambda s: s.activation(out=alog_s[:], in_=alog_s[:], func=AF.Exp),
             wait=[(WT, small_wt)])
        Q["s"].pop()
        SS.n -= 1
        aneg_tmp = A6[:, 0:QC * NS].rearrange("p (q n) -> p q n", q=QC)
        s_op(lambda s: s.activation(out=aneg_tmp, in_=alog_s[:], func=AF.Exp),
             wait=[(WT, small_wt)])
        v_op(lambda v: v.tensor_scalar(out=alog_s[:], in0=aneg_tmp, scalar1=-1.0,
                                       scalar2=None, op0=ALU.mult),
             wait=[(SS, SS.n)])
        ANEG = alog_s
        v_op(lambda v: v.tensor_scalar(out=halfb_s[:], in0=bdt_s[:], scalar1=0.5,
                                       scalar2=None, op0=ALU.mult),
             wait=[(WT, small_wt)])
        v_op(lambda v: v.memset(htail_s[:], 0.0))
        v_op(lambda v: v.memset(halo_s[:], 0.0))

        wslot_last_use = [0, 0, 0]
        woslot_last_use = [0, 0]
        cache_ready = {}
        lw_counter = [0]

        def wb_barrier_waits():
            return [(s, s.n) for s in WBSEMS.values() if s.n > 0]

        def load_w_cached(src_fp32, cache_ap, key, dst_flat, gsem, ysem,
                          last_use, extra=(), slot_key=None):
            n = cache_ap.free_size()
            dv = dst_flat[:, 0:n]
            wb = WBSEMS[slot_key]
            wbw = [(wb, wb.n)]
            if not cache_ready.get(key):
                wn = gdma(dv, src_fp32,
                          wait=[(ST, last_use)] + wbw + wb_barrier_waits()
                          + list(extra), sem=gsem)
                emit("g", lambda g, c=cache_ap, s=dv: g.dma_start(c, s),
                     wait=[(gsem, wn)], inc=[(wb, 16)])
                cache_ready[key] = True
                return gsem, wn
            lw_counter[0] += 1
            if lw_counter[0] % 2 == 0:
                wn = gdma(dv, cache_ap,
                          wait=[(ST, last_use)] + wbw + wb_barrier_waits()
                          + list(extra), sem=gsem)
                return gsem, wn
            wn = ydma(dv, cache_ap,
                      wait=[(ST, last_use)] + wbw + wb_barrier_waits()
                      + list(extra), sem=ysem)
            return ysem, wn

        def emit_ln(toff, tlen, xn_view, sq_view, par0=0):
            ntt = tlen // TT
            entry_sv, entry_ss, entry_st = SV.n, SS.n, ST.n
            pr_a, pr_b = pb[4][0:1, 0:TT], pb[5][0:1, 0:TT]
            tr = {"m": [0] * ntt, "ssq": [0] * ntt, "e1": [0] * ntt,
                  "e2": [0] * ntt, "var": [0] * ntt, "rs32": [0] * ntt,
                  "rsb": [0] * ntt, "mrs": [0] * ntt, "brs": [0] * ntt,
                  "bmrs": [0] * ntt, "t1": [0] * ntt, "xn": [0] * ntt,
                  "c1": [0] * ntt}

            def g(d, j):
                return tr[d][j] if j >= 0 else 0

            def emit_lag(j):
                par = (j + par0) % 2
                xbf = xn_view[:, :, j * TT:(j + 1) * TT]
                slot = sq_view[par]
                tr["brs"][j] = mm_group(pb[6][:, 0:TT],
                                        [(ones_row[:], RSB[par][0:1, 0:TT])],
                                        wait=[(SS, tr["rsb"][j]),
                                              (SV, g("t1", j - 1))])
                tr["bmrs"][j] = mm_group(pb[7][:, 0:TT],
                                         [(ones_row[:], RSB[par][0:1, TT:2 * TT])],
                                         wait=[(SV, tr["mrs"][j]),
                                               (SV, g("xn", j - 1))])
                tr["t1"][j] = v_op(lambda v, xbf=xbf, slot=slot: v.tensor_tensor(
                    out=slot, in0=xbf,
                    in1=pb[6][:, 0:TT].unsqueeze(1).broadcast_to([128, NCH_F, TT]),
                    op=ALU.mult),
                    wait=[(ST, tr["brs"][j]), (ST, tr["ssq"][j])])
                tr["xn"][j] = v_op(lambda v, xbf=xbf, slot=slot: v.tensor_tensor(
                    out=xbf, in0=slot,
                    in1=pb[7][:, 0:TT].unsqueeze(1).broadcast_to([128, NCH_F, TT]),
                    op=ALU.subtract),
                    wait=[(ST, tr["bmrs"][j])])

            for j in range(ntt):
                sl = slice(toff + j * TT, toff + (j + 1) * TT)
                xbf = xn_view[:, :, j * TT:(j + 1) * TT]
                w0 = ([(ST, entry_st), (SS, entry_ss), (IO, 16)]
                      if j == 0 else [])
                tr["c1"][j] = v_op(lambda v, xbf=xbf, sl=sl: v.tensor_copy(
                    xbf, RES[:, :, sl]), wait=w0)
            for j in range(ntt):
                par = (j + par0) % 2
                xbf = xn_view[:, :, j * TT:(j + 1) * TT]
                slot = sq_view[par]
                c2 = s_op(lambda s, slot=slot, xbf=xbf: s.activation(
                    out=slot, in_=xbf, func=AF.Square),
                    wait=[(SV, tr["c1"][j]), (SV, g("xn", j - 2)),
                          (ST, g("ssq", j - 2))])
                tr["m"][j] = mm_group(
                    pb[2][0:1, 0:TT],
                    [(oned_col[:], xbf[:, c, :]) for c in range(NCH_F)],
                    wait=([(SV, entry_sv)] if j == 0 else []) +
                    [(SV, tr["c1"][j]), (SS, g("e1", j - 1))])
                tr["ssq"][j] = mm_group(
                    pb[3][0:1, 0:TT],
                    [(oned_col[:], slot[:, c, :]) for c in range(NCH_F)],
                    wait=[(SS, c2), (SS, g("e2", j - 1))])
                tr["e1"][j] = s_op(lambda s, par=par: s.copy(
                    out=R_M[par], in_=pb[2][0:1, 0:TT]),
                    wait=[(ST, tr["m"][j]), (SV, g("mrs", j - 2))])
                tr["e2"][j] = s_op(lambda s: s.copy(out=R_A, in_=pb[3][0:1, 0:TT]),
                                   wait=[(ST, tr["ssq"][j]),
                                         (SV, g("var", j - 1))])
                v1 = v_op(lambda v, par=par: v.tensor_tensor(
                    out=pr_a, in0=R_M[par], in1=R_M[par], op=ALU.mult),
                    wait=[(SS, tr["e1"][j]), (SS, g("rs32", j - 1))])
                tr["var"][j] = v_op(lambda v: v.tensor_tensor(
                    out=pr_b, in0=R_A, in1=pr_a, op=ALU.subtract),
                    wait=[(SS, tr["e2"][j])])
                v1 = v_op(lambda v: v.tensor_scalar(
                    out=pr_a, in0=pr_b, scalar1=EPS_LN, scalar2=None,
                    op0=ALU.add))
                s1 = s_op(lambda s: s.activation(out=pr_b, in_=pr_a, func=AF.Ln),
                          wait=[(SV, v1)])
                tr["rsb"][j] = s_op(lambda s, par=par: s.activation(
                    out=RSB[par][0:1, 0:TT], in_=pr_b, func=AF.Exp, scale=-0.5),
                    wait=[(ST, g("brs", j - 2))])
                tr["rs32"][j] = s_op(lambda s: s.activation(
                    out=pr_a, in_=pr_b, func=AF.Exp, scale=-0.5))
                tr["mrs"][j] = v_op(lambda v, par=par: v.tensor_tensor(
                    out=RSB[par][0:1, TT:2 * TT], in0=R_M[par], in1=pr_a,
                    op=ALU.mult),
                    wait=[(SS, tr["rs32"][j]), (ST, g("bmrs", j - 2))])
                if j >= 1:
                    emit_lag(j - 1)
            emit_lag(ntt - 1)
            return {"xn_ready": tr["xn"][ntt - 1], "xn_tiles": list(tr["xn"]),
                    "pb6_free": tr["t1"][ntt - 1], "pb7_free": tr["xn"][ntt - 1]}

        def emit_routing(layer):
            gate_s = gate0_s if layer == 0 else gate1_s
            for j2 in range(NTI):
                prs = [(XN[:, c, j2 * 128:(j2 + 1) * 128], gate_s[:, c, :])
                       for c in range(NCH_F)]
                lg = mm_group(pb[4][:, j2 * E:(j2 + 1) * E], prs,
                              wait=[(SV, SV.n), (SS, SS.n)] if j2 == 0 else ())
            pb4v = pb[4][:, 0:NTI * E].rearrange("p (j e) -> p j e", e=E)
            if layer == 0:
                vv = v_op(lambda v: v.tensor_reduce(out=rt_s1[:, :, 0], in_=pb4v,
                                                    axis=AX.X, op=ALU.max),
                          wait=[(ST, lg)])
                sg = s_op(lambda s: s.activation(out=rt_s2[:, :, 0], in_=rt_s1[:, :, 0],
                                                 func=AF.Tanh, scale=0.5),
                          wait=[(SV, vv)])
                sgv = v_op(lambda v: v.tensor_scalar(out=rt_c[:, :, 0],
                                                     in0=rt_s2[:, :, 0],
                                                     scalar1=0.5, scalar2=0.5,
                                                     op0=ALU.mult, op1=ALU.add),
                           wait=[(SS, sg)])
                sg = sgv
                vv = v_op(lambda v: v.tensor_tensor(
                    out=rt_a, in0=pb4v,
                    in1=rt_s1.broadcast_to([128, NTI, E]), op=ALU.is_ge))
                pb4_free = [(SV, vv)]
                vfin = v_op(lambda v: v.tensor_tensor(
                    out=rt_bf, in0=rt_a,
                    in1=rt_c[:, :, 0:1].broadcast_to([128, NTI, E]), op=ALU.mult),
                    wait=[(SV, sg)])
            else:
                sg = s_op(lambda s: s.activation(out=rt_b, in_=pb4v,
                                                 func=AF.Tanh, scale=0.5),
                          wait=[(ST, lg)])
                pb4_free = [(SS, sg)]
                sg = v_op(lambda v: v.tensor_scalar(out=rt_sc, in0=rt_b,
                                                    scalar1=0.5, scalar2=0.5,
                                                    op0=ALU.mult, op1=ALU.add),
                          wait=[(SS, sg)])
                vv = v_op(lambda v: v.tensor_tensor(
                    out=rt_a, in0=rt_sc,
                    in1=cbias_s[:].unsqueeze(1).broadcast_to([128, NTI, E]),
                    op=ALU.add), wait=[(SV, sg)])
                rta4 = rt_a.rearrange("p j (g two) -> p j g two", two=2)
                vv = v_op(lambda v: v.tensor_tensor(out=rt_g, in0=rta4[:, :, :, 0],
                                                    in1=rta4[:, :, :, 1], op=ALU.add))
                vv = v_op(lambda v: v.tensor_tensor(out=rt_s1[:, :, 0], in0=rt_g[:, :, 0],
                                                    in1=rt_g[:, :, 1], op=ALU.max))
                vv = v_op(lambda v: v.tensor_tensor(out=rt_s2[:, :, 0], in0=rt_g[:, :, 2],
                                                    in1=rt_g[:, :, 3], op=ALU.max))
                vv = v_op(lambda v: v.tensor_tensor(out=rt_h[:, :, 0], in0=rt_s1[:, :, 0],
                                                    in1=rt_s2[:, :, 0], op=ALU.min))
                vv = v_op(lambda v: v.tensor_tensor(out=rt_s1[:, :, 0], in0=rt_g[:, :, 0],
                                                    in1=rt_g[:, :, 1], op=ALU.min))
                vv = v_op(lambda v: v.tensor_tensor(out=rt_s2[:, :, 0], in0=rt_g[:, :, 2],
                                                    in1=rt_g[:, :, 3], op=ALU.min))
                vv = v_op(lambda v: v.tensor_tensor(out=rt_h[:, :, 1], in0=rt_s1[:, :, 0],
                                                    in1=rt_s2[:, :, 0], op=ALU.max))
                vv = v_op(lambda v: v.tensor_tensor(out=rt_s2[:, :, 0], in0=rt_h[:, :, 0],
                                                    in1=rt_h[:, :, 1], op=ALU.max))
                vv = v_op(lambda v: v.tensor_tensor(
                    out=rt_h, in0=rt_g,
                    in1=rt_s2.broadcast_to([128, NTI, 4]), op=ALU.is_ge))
                rt_b4 = rt_b.rearrange("p j (g two) -> p j g two", two=2)
                rt_c4 = rt_c.rearrange("p j (g two) -> p j g two", two=2)
                emv = rt_h.unsqueeze(3).broadcast_to([128, NTI, 4, 2])
                vv = v_op(lambda v: v.tensor_tensor(out=rt_b4, in0=rta4, in1=emv,
                                                    op=ALU.mult))
                vv = v_op(lambda v: v.tensor_scalar(out=rt_c4, in0=emv, scalar1=1.0,
                                                    scalar2=None, op0=ALU.subtract))
                vv = v_op(lambda v: v.tensor_tensor(
                    out=rt_a, in0=rt_c,
                    in1=negeps_s[:].unsqueeze(1).broadcast_to([128, NTI, E]),
                    op=ALU.mult))
                vv = v_op(lambda v: v.tensor_tensor(out=rt_c, in0=rt_b, in1=rt_a,
                                                    op=ALU.add))
                vv = v_op(lambda v: v.tensor_tensor(
                    out=rt_big,
                    in0=rt_c.unsqueeze(2).broadcast_to([128, NTI, E, E]),
                    in1=rt_c.unsqueeze(3).broadcast_to([128, NTI, E, E]),
                    op=ALU.is_gt))
                vv = v_op(lambda v: v.tensor_reduce(out=rt_a, in_=rt_big,
                                                    axis=AX.X, op=ALU.add))
                vv = v_op(lambda v: v.tensor_scalar(out=rt_b, in0=rt_a, scalar1=3.5,
                                                    scalar2=None, op0=ALU.is_lt))
                vv = v_op(lambda v: v.tensor_tensor(out=rt_a, in0=rt_b, in1=rt_sc,
                                                    op=ALU.mult))
                vv = v_op(lambda v: v.tensor_reduce(out=rt_s1[:, :, 0], in_=rt_a,
                                                    axis=AX.X, op=ALU.add))
                vv = v_op(lambda v: v.tensor_scalar(out=rt_s2[:, :, 0], in0=rt_s1[:, :, 0],
                                                    scalar1=1e-20, scalar2=None,
                                                    op0=ALU.add))
                vv = v_op(lambda v: v.reciprocal(out=rt_s1[:, :, 0], in_=rt_s2[:, :, 0]))
                vv = v_op(lambda v: v.tensor_scalar(out=rt_s2[:, :, 0], in0=rt_s1[:, :, 0],
                                                    scalar1=2.5, scalar2=None,
                                                    op0=ALU.mult))
                vfin = v_op(lambda v: v.tensor_tensor(
                    out=rt_bf, in0=rt_a,
                    in1=rt_s2.broadcast_to([128, NTI, E]), op=ALU.mult))
            for j2 in range(NTI):
                tp = t_op(lambda t, j2=j2: t.transpose(
                    pbf[5][0:E, 0:128], rt_bf[:, j2, :], id_bf[:]),
                    wait=[(SV, vfin), (SS, SS.n)])
                s_op(lambda s, j2=j2: s.copy(
                    out=scale_fm[:, j2 * 128:(j2 + 1) * 128], in_=pbf[5][0:E, 0:128]),
                    wait=[(ST, tp)])
            return SS.n, pb4_free

        def emit_ffn(layer):
            prev_v, prev_s = SV.n, SS.n
            lnr = emit_ln(0, T, XN, SQ2)
            xn_ready = lnr["xn_ready"]
            sc_done, pb4_free = emit_routing(layer)
            wg_h_, wu_h_, wd_h_ = ((wg0_h, wu0_h, wd0_h) if layer == 0
                                   else (wg1_h, wu1_h, wd1_h))
            has_shared = (layer == 1)
            nexp = E + (1 if has_shared else 0)
            mat = [0]

            wc = wc_l0 if layer == 0 else wc_l1

            def load_w(src_ap, key_idx):
                slot = mat[0] % 3
                sem, wn = load_w_cached(src_ap, wc[key_idx], (layer, key_idx),
                                        SLOT3[slot], WSEM3[slot], YWSEM[slot],
                                        wslot_last_use[slot],
                                        slot_key=("ws", slot))
                mat[0] += 1
                return slot, sem, wn

            XS1 = A4[:, 4096:4096 + NCH_F * TT].rearrange(
                "p (c t) -> p c t", c=NCH_F)
            SLOT3 = [WSLOT[0], WSLOT[1], A4[:, 0:4096]]
            WSEM3 = [WSEM[0], WSEM[1], WS2]
            gbank_free = [prev_s, prev_s]
            ubank_free = [xn_ready, xn_ready]
            bc_free = [pb4_free, [(SS, sc_done)]]
            screp_free = [0, 0]
            silub_free = [0, 0]
            down_free_v = [0] * 8
            rscr_free = [prev_s, prev_s]
            st_state = {"pstore_free": 0, "xs_free": [0, 0],
                        "xs_ready": [xn_ready, xn_ready],
                        "screp_ready": [0, 0], "pdone": [0] * nexp}

            def prep_scale(e, tsl):
                par = e % 2
                bcast = mm_group(
                    pb[4 + par][:, 0:TT],
                    [(esel8_s[:, e * 128:(e + 1) * 128], scale_fm[:, tsl])],
                    wait=([(SS, sc_done)] +
                          (list(bc_free[par]) if isinstance(bc_free[par], list)
                           else [(SS, bc_free[par])])))
                scv = s_op(lambda s, par=par: s.copy(out=SCREP[par],
                                                     in_=pb[4 + par][:, 0:TT]),
                           wait=[(ST, bcast), (SS, sc_done),
                                 (SV, screp_free[par])])
                bc_free[par] = scv
                st_state["screp_ready"][par] = scv

            for j in range(NTT):
                tsl = slice(j * TT, (j + 1) * TT)
                prep_scale(0, tsl)
                for e in range(nexp):
                    par = e % 2
                    shared = (e == E)
                    if not shared:
                        gsrc = wg_h_[e].rearrange("p c i -> p (c i)")
                        usrc = wu_h_[e].rearrange("p c i -> p (c i)")
                    else:
                        gsrc = wgs_h[:].rearrange("p c i -> p (c i)")
                        usrc = wus_h[:].rearrange("p c i -> p (c i)")
                    gslot, gsem, gwn = load_w(gsrc, e * 2)
                    uslot, usem, uwn = load_w(usrc, e * 2 + 1)
                    gv = SLOT3[gslot].rearrange("p (c i) -> p c i", c=NCH_F)
                    uv = SLOT3[uslot].rearrange("p (c i) -> p c i", c=NCH_F)
                    if layer == 0 and not shared:
                        xsv = v_op(lambda v, par=par, tsl=tsl: v.tensor_tensor(
                            out=XS1[:, :, 0:TT], in0=XN[:, :, tsl],
                            in1=SCREP[par].unsqueeze(1).broadcast_to(
                                [128, NCH_F, TT]), op=ALU.mult),
                            wait=[(SS, st_state["screp_ready"][par]),
                                  (SV, xn_ready), (ST, ST.n)])
                        screp_free[par] = xsv
                        rhs_src, rhs_rdy = XS1[:, :, 0:TT], xsv
                    else:
                        rhs_src, rhs_rdy = XN[:, :, tsl], xn_ready
                    for ic in range(4):
                        gps = pb[ic % 2][:, 0:TT]
                        gdn = mm_group(gps, [(gv[:, c, ic * 128:(ic + 1) * 128],
                                              rhs_src[:, c, :])
                                             for c in range(NCH_F)],
                                       wait=[(gsem, gwn),
                                             (SS, gbank_free[ic % 2]),
                                             (SV, rhs_rdy)], split=True)
                        ups = pb[2 + ic % 2][:, 0:TT]
                        udn = mm_group(ups, [(uv[:, c, ic * 128:(ic + 1) * 128],
                                              rhs_src[:, c, :])
                                             for c in range(NCH_F)],
                                       wait=[(usem, uwn),
                                             (SV, ubank_free[ic % 2])],
                                       split=True)
                        if ic == 0 and e + 1 < nexp and e + 1 < E:
                            prep_scale(e + 1, tsl)
                        sil = s_op(lambda s, ic=ic, gps=gps: s.activation(
                            out=SILUB[ic % 2], in_=gps, func=AF.Silu),
                            wait=[(ST, gdn), (SV, silub_free[ic % 2])])
                        gbank_free[ic % 2] = sil
                        if layer == 0 or shared:
                            dst = (PS_SH[:, ic, 0:TT] if shared
                                   else PSTORE[:, e, ic, 0:TT])
                            pv = v_op(lambda v, ic=ic, ups=ups, dst=dst:
                                      v.tensor_tensor(out=dst, in0=SILUB[ic % 2],
                                                      in1=ups, op=ALU.mult),
                                      wait=[(SS, sil), (ST, udn),
                                            (ST, st_state["pstore_free"])])
                            ubank_free[ic % 2] = pv
                            silub_free[ic % 2] = pv
                        else:
                            pv = v_op(lambda v, ic=ic, ups=ups: v.tensor_tensor(
                                out=PTMP, in0=SILUB[ic % 2], in1=ups,
                                op=ALU.mult),
                                wait=[(SS, sil), (ST, udn),
                                      (SS, st_state["screp_ready"][par])])
                            ubank_free[ic % 2] = pv
                            silub_free[ic % 2] = pv
                            pv = v_op(lambda v, e=e, ic=ic, par=par:
                                      v.tensor_tensor(
                                          out=PSTORE[:, e, ic, 0:TT], in0=PTMP,
                                          in1=SCREP[par], op=ALU.mult),
                                      wait=[(ST, st_state["pstore_free"])])
                            if ic == 3:
                                screp_free[par] = pv
                    st_state["pdone"][e] = SV.n
                    wslot_last_use[gslot] = ST.n
                    wslot_last_use[uslot] = ST.n
                    if layer == 0 and not shared:
                        st_state["xs_free"][par] = ST.n
                dn_bank = [0] * 8
                for e in range(nexp):
                    shared = (e == E)
                    dsrc = (wds_h[:] if shared else wd_h_[e]).rearrange(
                        "p i d -> p (i d)")
                    dslot, dsem, dwn = load_w(dsrc, (16 + e) if layer == 0 else (18 + e))
                    dv = SLOT3[dslot].rearrange("p (i d) -> p i d", i=4)
                    for d in range(8):
                        prs = []
                        for ic in range(4):
                            for hh in range(2):
                                hs = slice(hh * HALF, (hh + 1) * HALF)
                                psrc = (PS_SH[:, ic, hs] if shared
                                        else PSTORE[:, e, ic, hs])
                                prs.append((dv[:, ic, d * 128:(d + 1) * 128],
                                            psrc, hh))

                        def fn(t, prs=prs, d=d, first=(e == 0)):
                            r = None
                            for i, (lh, rh, hh) in enumerate(prs):
                                r = t.matmul(pb[d][:, hh * HALF:(hh + 1) * HALF],
                                             lh, rh, start=(first and i == 0),
                                             stop=False, skip_group_check=True)
                            return r
                        w = [(dsem, dwn), (SV, st_state["pdone"][e])]
                        if e == 0:
                            if d < 2:
                                w.append((SS, gbank_free[d]))
                            elif d < 4:
                                w.append((SV, ubank_free[d - 2]))
                            elif d < 6:
                                bf = bc_free[d - 4]
                                w += (list(bf) if isinstance(bf, list)
                                      else [(SS, bf)])
                            else:
                                w += [(SV, xn_ready)]
                            w.append((SV, down_free_v[d]))
                        emit("t", fn, wait=w, inc=[(ST, 1)])
                        dn_bank[d] = ST.n
                    wslot_last_use[dslot] = ST.n
                for d in range(8):
                    rv = v_op(lambda v, d=d, tsl=tsl: v.scalar_tensor_tensor(
                        out=RSCR[d % 2], in0=pb[d][:, 0:TT], scalar=0.5,
                        in1=RES[:, d, tsl], op0=ALU.mult, op1=ALU.add),
                        wait=[(ST, dn_bank[d]), (SS, rscr_free[d % 2]),
                              (SS, prev_s)])
                    down_free_v[d] = rv
                    cp = s_op(lambda s, d=d, tsl=tsl: s.copy(out=RES[:, d, tsl],
                                                             in_=RSCR[d % 2]),
                              wait=[(SV, rv)])
                    rscr_free[d % 2] = cp
                st_state["pstore_free"] = ST.n

        def emit_l0attn():
            xn_ready = emit_ln(0, T, XN, SQ2)["xn_ready"]
            wna = gdma(wq_s, wq_h[:], wait=[(ST, ST.n), (SV, SV.n), (SS, SS.n)],
                       sem=WSEM[0])
            wnb = gdma(wk_s, wk_h[:], sem=WSEM[1])
            wnc = gdma(wv_s, wv_h[:], sem=WOSEM[0])
            wnd = gdma(wo_s, wo_h[:], sem=WOSEM[1])
            w0waits = [(WSEM[0], wna), (WSEM[1], wnb), (WOSEM[0], wnc),
                       (WOSEM[1], wnd)]
            a4wb = wb_barrier_waits()
            for j in range(NTT):
                tsl = slice(j * TT, (j + 1) * TT)
                for pi, (wsb, elu) in enumerate([(wq_s, True), (wk_s, True),
                                                 (wv_s, False)]):
                    ps = pb[pi % 2][0:KQ, 0:TT]
                    dn = mm_group(ps, [(wsb[:, c, :], XN[:, c, tsl])
                                       for c in range(NCH_F)],
                                  wait=[(SV, SV.n), (SS, SS.n)] + w0waits,
                                  split=True)
                    if elu:
                        dst = QFM[:, tsl] if pi == 0 else KFM[:, tsl]
                        va = v_op(lambda v, ps=ps: v.tensor_scalar(
                            out=ELU_A, in0=ps, scalar1=0.0, scalar2=None,
                            op0=ALU.max), wait=[(ST, dn)])
                        va = v_op(lambda v, ps=ps: v.tensor_scalar(
                            out=ELU_B[:, 0:TT] if False else ELU_B, in0=ps,
                            scalar1=0.0, scalar2=None, op0=ALU.min))
                        sa = s_op(lambda s: s.activation(out=ELU_C, in_=ELU_B,
                                                         func=AF.Exp),
                                  wait=[(SV, va)])
                        v_op(lambda v, dst=dst: v.tensor_tensor(
                            out=dst, in0=ELU_A, in1=ELU_C, op=ALU.add),
                            wait=[(SS, sa)] + (a4wb if (j == 0 and pi == 0)
                                               else []))
                    else:
                        s_op(lambda s, ps=ps, tsl=tsl: s.copy(out=VFM[:, tsl],
                                                              in_=ps),
                             wait=[(ST, dn)] + (a4wb if j == 0 else []))
            wslot_last_use[0] = wslot_last_use[1] = ST.n
            woslot_last_use[0] = woslot_last_use[1] = ST.n
            for c in range(NTI):
                csl = slice(c * 128, (c + 1) * 128)
                tp = t_op(lambda t, csl=csl: t.transpose(
                    pbf[5][:, 0:KQ], KFM[:, csl], id_bf[0:KQ, 0:KQ]),
                    wait=[(SV, SV.n), (SS, SS.n)])
                s_op(lambda s, c=c: s.copy(out=KTOK[:, c, :], in_=pbf[5][:, 0:KQ]),
                     wait=[(ST, tp)])
                tp = t_op(lambda t, csl=csl: t.transpose(
                    pbf[5][:, KQ:2 * KQ], VFM[:, csl], id_bf[0:KQ, 0:KQ]),
                    wait=[(SS, SS.n)])
                s_op(lambda s, c=c: s.copy(out=VTOK[:, c, :],
                                           in_=pbf[5][:, KQ:2 * KQ]),
                     wait=[(ST, tp)])
            v_op(lambda v: v.memset(sst_s[:, 0, :], 0.0))
            vz = v_op(lambda v: v.memset(zst_s[:, 0, :], 0.0))
            sc = s_op(lambda s: s.copy(out=sbf_s[:], in_=sst_s[:, 0, :]),
                      wait=[(SV, vz)])
            sc = s_op(lambda s: s.copy(out=zbf_s[:], in_=zst_s[:, 0, :]))
            for c in range(NTI):
                par = c % 2
                csl = slice(c * 128, (c + 1) * 128)
                at = mm_group(pb[2][:, 0:128], [(KFM[:, csl], QFM[:, csl])],
                              wait=[(SV, SV.n)])
                va = v_op(lambda v: v.tensor_tensor(out=atm_s[:], in0=pb[2][:, 0:128],
                                                    in1=causalT[:], op=ALU.mult),
                          wait=[(ST, at)])
                nm = mm_group(pb[3][:, 0:KQ],
                              [(atm_s[:], VTOK[:, c, :]), (QFM[:, csl], sbf_s[:])],
                              wait=[(SV, va), (SS, SS.n)])
                nm = mm_group(pb[3][:, KQ:KQ + 1],
                              [(atm_s[:], ones_col[:]), (QFM[:, csl], zbf_s[:])])
                ds = mm_group(pb[6][0:KQ, 0:KQ],
                              [(KTOK[:, c, :], VTOK[:, c, :])])
                ds = mm_group(pb[6][0:KQ, KQ:KQ + 1], [(KTOK[:, c, :], ones_col[:])])
                vs = v_op(lambda v, par=par: v.tensor_tensor(
                    out=sst_s[:, 1 - par, :], in0=sst_s[:, par, :],
                    in1=pb[6][0:KQ, 0:KQ], op=ALU.add), wait=[(ST, ds)])
                vs = v_op(lambda v, par=par: v.tensor_tensor(
                    out=zst_s[:, 1 - par, :], in0=zst_s[:, par, :],
                    in1=pb[6][0:KQ, KQ:KQ + 1], op=ALU.add))
                s_op(lambda s, par=par: s.copy(out=sbf_s[:], in_=sst_s[:, 1 - par, :]),
                     wait=[(SV, vs)])
                s_op(lambda s, par=par: s.copy(out=zbf_s[:], in_=zst_s[:, 1 - par, :]))
                vd = v_op(lambda v: v.tensor_scalar(out=den_s[:], in0=pb[3][:, KQ:KQ + 1],
                                                    scalar1=1e-6, scalar2=None,
                                                    op0=ALU.add), wait=[(ST, nm)])
                vd = v_op(lambda v: v.reciprocal(out=rcp_s[:], in_=den_s[:]))
                vo = v_op(lambda v: v.tensor_scalar(out=otok_s, in0=pb[3][:, 0:KQ],
                                                    scalar1=rcp_s[:], scalar2=None,
                                                    op0=ALU.mult))
                tp = t_op(lambda t: t.transpose(pbf[5][0:KQ, 128:256], otok_s,
                                                id_bf[:]), wait=[(SV, vo)])
                so = s_op(lambda s, c=c: s.copy(
                    out=OFM[:, (c % (TT // 128)) * 128:(c % (TT // 128)) * 128 + 128],
                    in_=pbf[5][0:KQ, 128:256]), wait=[(ST, tp)])
                if (c + 1) % (TT // 128) == 0:
                    j = c // (TT // 128)
                    tsl = slice(j * TT, (j + 1) * TT)
                    for d in range(8):
                        wod = mm_group(pb[7][:, 0:TT],
                                       [(wo_s[:, d * 128:(d + 1) * 128], OFM[:, 0:TT])],
                                       wait=[(SS, so), (SV, SV.n)])
                        rv = v_op(lambda v, d=d, tsl=tsl: v.scalar_tensor_tensor(
                            out=RSCR[d % 2], in0=pb[7][:, 0:TT], scalar=1.0,
                            in1=RES[:, d, tsl], op0=ALU.mult, op1=ALU.add),
                            wait=[(ST, wod), (SS, SS.n)])
                        so2 = s_op(lambda s, d=d, tsl=tsl: s.copy(
                            out=RES[:, d, tsl], in_=RSCR[d % 2]), wait=[(SV, rv)])
                    so = so2
            wslot_last_use[2] = ST.n
            woslot_last_use[0] = woslot_last_use[1] = ST.n

        def emit_l1attn():
            aneg_flat = alog_s[:].rearrange("p q n -> p (q n)")
            a3_flat = A6[:, 1536:1792]
            v_op(lambda v: v.tensor_scalar(out=a3_flat, in0=aneg_flat,
                                           scalar1=0.0505, scalar2=None,
                                           op0=ALU.mult),
                 wait=[(SV, SV.n), (SS, SS.n), (ST, ST.n)])
            v_op(lambda v: v.tensor_scalar(out=aneg_flat, in0=a3_flat,
                                           scalar1=0.0495 / 0.0505,
                                           scalar2=None, op0=ALU.mult))
            prev_wout = None
            wout_rv_free = [0, 0]
            prev_conv_tail = [0, 0]
            wout_scopy = [0]
            woslot_last_use[0] = max(woslot_last_use[0], ST.n)
            woslot_last_use[1] = max(woslot_last_use[1], ST.n)
            for k in range(NCHK):
                koff = k * TCH
                ksl = slice(koff, koff + TCH)
                xn_ready = emit_ln(koff, TCH, XNC,
                                   [L1SQ[:, :, 0:TT], L1SQ[:, :, 0:TT]],
                                   par0=1)["xn_ready"]
                if k == 0:
                    v_op(lambda v: v.memset(XBRS[:, :, 0:3], 0.0))
                if prev_wout is not None:
                    prev_wout()
                    prev_wout = None
                zs_done = [0] * QC
                xz_bank_free = [SS.n, SS.n]

                def xz_load(grp):
                    slot = grp % 2
                    wsem, wn = load_w_cached(
                        win_h[:, :, grp * 512:(grp + 1) * 512], wc_win[grp],
                        ("win", grp), WSLOT[slot], WSEM[slot], YWSEM[slot],
                        wslot_last_use[slot], slot_key=("ws", slot))
                    gv = WSLOT[slot].rearrange("p (c i) -> p c i", c=NCH_F)
                    return slot, wsem, wn, gv

                def xz_mm(gv, wsem, wn, m, mj):
                    ps = pb[m % 2][:, 0:TCH]
                    dn = mm_group(ps, [(gv[:, cc, mj * 128:(mj + 1) * 128],
                                        XNC[:, cc, 0:TCH])
                                       for cc in range(NCH_F)],
                                  wait=[(wsem, wn), (SV, xn_ready),
                                        (SS, xz_bank_free[m % 2]),
                                        (SV, wout_rv_free[m % 2])], split=True)
                    return ps, dn

                conv_done = [0] * QC
                for grp in range(4):
                    slot, wsem, wn, gv = xz_load(grp)
                    for mj in range(4):
                        m = grp * 4 + mj
                        ps, dn = xz_mm(gv, wsem, wn, m, mj)
                        q = m
                        sq_ = q % 2
                        hv = v_op(lambda v, q=q, sq_=sq_: v.tensor_copy(
                            XBRS[:, sq_, 0:3], halo_s[:, q, :]),
                            wait=[(SV, conv_done[q - 2] if q >= 2
                                   else prev_conv_tail[q])])
                        ev = s_op(lambda s, ps=ps, sq_=sq_: s.copy(
                            out=XBRS[:, sq_, 3:3 + TCH], in_=ps),
                            wait=[(ST, dn), (SV, hv)])
                        xz_bank_free[q % 2] = ev
                        cv = v_op(lambda v, q=q, sq_=sq_: v.tensor_scalar(
                            out=CVA, in0=XBRS[:, sq_, 0:TCH],
                            scalar1=convw_s[:, q, 0:1],
                            scalar2=convb_s[:, q:q + 1],
                            op0=ALU.mult, op1=ALU.add), wait=[(SS, ev)])
                        cv = v_op(lambda v, q=q, sq_=sq_: v.scalar_tensor_tensor(
                            out=CVB, in0=XBRS[:, sq_, 1:1 + TCH],
                            scalar=convw_s[:, q, 1:2], in1=CVA,
                            op0=ALU.mult, op1=ALU.add))
                        cv = v_op(lambda v, q=q, sq_=sq_: v.scalar_tensor_tensor(
                            out=CVA, in0=XBRS[:, sq_, 2:2 + TCH],
                            scalar=convw_s[:, q, 2:3], in1=CVB,
                            op0=ALU.mult, op1=ALU.add))
                        cv = v_op(lambda v, q=q, sq_=sq_: v.scalar_tensor_tensor(
                            out=ZS[:, q, :], in0=XBRS[:, sq_, 3:3 + TCH],
                            scalar=convw_s[:, q, 3:4], in1=CVA,
                            op0=ALU.mult, op1=ALU.add))
                        cv = v_op(lambda v, q=q, sq_=sq_: v.tensor_copy(
                            halo_s[:, q, :], XBRS[:, sq_, TCH:TCH + 3]))
                        conv_done[q] = cv
                    wslot_last_use[slot] = ST.n
                prev_conv_tail = [conv_done[QC - 2], conv_done[QC - 1]]
                wout_done_st = ST.n
                cv_done = [0] * QC
                for q in range(QC):
                    cvs = s_op(lambda s, q=q: s.activation(
                        out=XCV[:, q, :], in_=ZS[:, q, :], func=AF.Silu),
                        wait=[(SV, conv_done[q]), (ST, wout_done_st)])
                    cv_done[q] = cvs
                xcv_done = SS.n
                xd = mm_group(pb[2][0:96, 0:TCH],
                              [(wx_s[:, q, :], XCV[:, q, :]) for q in range(QC)],
                              wait=[(SS, xcv_done), (SV, SV.n)], split=True)
                vmz = v_op(lambda v: v.memset(xdbl_bf[96:128, 0:TCH], 0.0),
                           wait=[(ST, xd)])
                xde = s_op(lambda s: s.copy(out=xdbl_bf[0:96, 0:TCH],
                                            in_=pb[2][0:96, 0:TCH]),
                           wait=[(ST, xd)])
                for n in range(NS):
                    bb = mm_group(pb[3 + n % 2][:, 0:TCH],
                                  [(nsel_s[:, n * 128:(n + 1) * 128],
                                    xdbl_bf[:, 0:TCH])],
                                  wait=[(SS, xde), (SV, vmz), (SS, SS.n),
                                        (SV, SV.n)])
                    s_op(lambda s, n=n, bb=bb: s.copy(out=BREP[:, n, :],
                                                      in_=pb[3 + n % 2][:, 0:TCH]),
                         wait=[(ST, bb)])
                for n in range(NS):
                    cc_ = mm_group(pb[3 + n % 2][:, 0:TCH],
                                   [(nsel_s[:, (16 + n) * 128:(17 + n) * 128],
                                     xdbl_bf[:, 0:TCH])],
                                   wait=[(SS, SS.n)])
                    s_op(lambda s, n=n, cc_=cc_: s.copy(out=CREP[:, n, :],
                                                        in_=pb[3 + n % 2][:, 0:TCH]),
                         wait=[(ST, cc_)])
                brep_done = SS.n
                for grp in range(4, 8):
                    slot, wsem, wn, gv = xz_load(grp)
                    for mj in range(4):
                        m = grp * 4 + mj
                        ps, dn = xz_mm(gv, wsem, wn, m, mj)
                        q = m - QC
                        zq = s_op(lambda s, ps=ps, q=q: s.activation(
                            out=ZS[:, q, :], in_=ps, func=AF.Silu),
                            wait=[(ST, dn)])
                        zs_done[q] = zq
                        xz_bank_free[m % 2] = zq
                    wslot_last_use[slot] = ST.n
                a4_free_st = ST.n
                dtbank_free = [SS.n, SS.n]
                daslot_free = [SV.n, SV.n]
                dtq_free = [SV.n, SV.n]
                for q in range(QC):
                    dtp = 5 if q % 2 == 0 else 6
                    dt_mm = mm_group(pb[dtp][:, 0:TCH],
                                     [(wdt_s[:, q * 128:(q + 1) * 128],
                                       xdbl_bf[0:64, 0:TCH])],
                                     wait=[(SS, dtbank_free[q % 2]),
                                           (SS, xde)])
                    sdt = s_op(lambda s, dtp=dtp, q=q: s.activation(
                        out=DTQS[q % 2], in_=pb[dtp][:, 0:TCH], func=AF.Tanh,
                        scale=0.5, bias=halfb_s[:, q:q + 1]),
                        wait=[(ST, dt_mm), (SV, dtq_free[q % 2])])
                    dtbank_free[q % 2] = sdt
                    vdt = v_op(lambda v, q=q: v.tensor_scalar(
                        out=dtbf_s, in0=DTQS[q % 2], scalar1=0.0495,
                        scalar2=0.0505, op0=ALU.mult, op1=ALU.add),
                        wait=[(SS, sdt)])
                    dtq_free[q % 2] = vdt
                    vdx = v_op(lambda v, q=q: v.tensor_tensor(
                        out=dtx_s, in0=dtbf_s, in1=XCV[:, q, :],
                        op=ALU.mult), wait=[(SS, cv_done[q])])
                    for g8 in range(2):
                        slot = (q * 2 + g8) % 2
                        nsl = slice(g8 * 8, (g8 + 1) * 8)
                        for nn in range(8):
                            w = ([(SV, daslot_free[slot]),
                                  (WBSEMS[("ws", slot)], WBSEMS[("ws", slot)].n)]
                                 if nn == 0 else ())
                            s_op(lambda s, slot=slot, nn=nn, q=q, g8=g8:
                                 s.activation(
                                     out=DASLOT[slot][:, nn, :],
                                     in_=DTQS[q % 2], func=AF.Exp,
                                     scale=A2P[:, q, g8 * 8 + nn:g8 * 8 + nn + 1],
                                     bias=A3P[:, q, g8 * 8 + nn:g8 * 8 + nn + 1]),
                                 wait=w)
                        da_done = SS.n
                        uw = [(SV, vdx), (SS, brep_done)]
                        if q == 0 and g8 == 0:
                            uw.append((ST, a4_free_st))
                            uw += wb_barrier_waits()
                        uv = v_op(lambda v, nsl=nsl: v.tensor_tensor(
                            out=UBUF, in0=dtx_s.unsqueeze(1).broadcast_to(
                                [128, 8, TCH]),
                            in1=BREP[:, nsl, :], op=ALU.mult),
                            wait=uw)
                        v_op(lambda v, slot=slot: v.tensor_copy(
                            fixa[:, 0:8], DASLOT[slot][:, :, 0]),
                            wait=[(SS, da_done)])
                        v_op(lambda v, q=q, nsl=nsl: v.tensor_tensor(
                            out=fixb[:, 0:8], in0=fixa[:, 0:8],
                            in1=htail_s[:, q, nsl], op=ALU.mult))
                        v_op(lambda v: v.tensor_tensor(
                            out=fixc[:, 0:8], in0=fixb[:, 0:8],
                            in1=UBUF[:, :, 0], op=ALU.add))
                        v_op(lambda v: v.tensor_copy(UBUF[:, :, 0], fixc[:, 0:8]))
                        v_op(lambda v, slot=slot: v.memset(DASLOT[slot][:, :, 0],
                                                           0.0))
                        scn = v_op(lambda v, slot=slot: v.tensor_tensor_scan(
                            out=HBUF.rearrange("p n t -> p (n t)"),
                            data0=DASLOT[slot].rearrange("p n t -> p (n t)"),
                            data1=UBUF.rearrange("p n t -> p (n t)"),
                            initial=0.0, op0=ALU.mult, op1=ALU.add))
                        daslot_free[slot] = scn
                        v_op(lambda v, q=q, nsl=nsl: v.tensor_copy(
                            htail_s[:, q, nsl], HBUF[:, :, TCH - 1]))
                        v_op(lambda v, nsl=nsl: v.tensor_tensor(
                            out=MBUF, in0=HBUF, in1=CREP[:, nsl, :], op=ALU.mult))
                        v_op(lambda v: v.tensor_tensor(
                            out=MB2, in0=MBUF[:, 0:4, :], in1=MBUF[:, 4:8, :],
                            op=ALU.add),
                            wait=([(SS, wout_scopy[0])]
                                  if (q == 0 and g8 == 0) else ()))
                        v_op(lambda v: v.tensor_tensor(
                            out=MBUF[:, 0:2, :], in0=MB2[:, 0:2, :],
                            in1=MB2[:, 2:4, :], op=ALU.add))
                        dst = YTOT if g8 == 0 else G1B
                        v_op(lambda v, dst=dst: v.tensor_tensor(
                            out=dst, in0=MBUF[:, 0, :], in1=MBUF[:, 1, :],
                            op=ALU.add))
                    v_op(lambda v: v.tensor_tensor(out=MB2[:, 0, :], in0=YTOT,
                                                   in1=G1B, op=ALU.add))
                    v_op(lambda v, q=q: v.scalar_tensor_tensor(
                        out=YTOT, in0=XCV[:, q, :], scalar=dpar_s[:, q:q + 1],
                        in1=MB2[:, 0, :], op0=ALU.mult, op1=ALU.add))
                    v_op(lambda v, q=q: v.tensor_tensor(
                        out=XCV[:, q, :], in0=YTOT, in1=ZS[:, q, :], op=ALU.mult),
                        wait=[(SS, zs_done[q])])
                yg_done = SV.n

                def make_wout(k=k, ksl=ksl, yg_done=yg_done):
                    def emit_wout():
                        for grp in range(4):
                            slot = grp % 2
                            wsem2, wn = load_w_cached(
                                wout_h[:, :, grp * 256:(grp + 1) * 256],
                                wc_wout[grp], ("wout", grp),
                                WOSLOT[slot].rearrange("p q d -> p (q d)"),
                                WOSEM[slot], YWSEM[slot], woslot_last_use[slot],
                                extra=[(SV, yg_done)], slot_key=("wo", slot))
                            for dl in range(2):
                                d = grp * 2 + dl
                                wod = mm_group(
                                    pb[dl % 2][:, 0:TCH],
                                    [(WOSLOT[slot][:, q, dl * 128:(dl + 1) * 128],
                                      XCV[:, q, :]) for q in range(QC)],
                                    wait=[(wsem2, wn), (SV, yg_done),
                                          (SS, SS.n)], split=True)
                                rv = v_op(lambda v, d=d, dl=dl, ksl=ksl:
                                          v.scalar_tensor_tensor(
                                              out=RSCR1[dl % 2],
                                              in0=pb[dl % 2][:, 0:TCH],
                                              scalar=1.0, in1=RES[:, d, ksl],
                                              op0=ALU.mult, op1=ALU.add),
                                          wait=[(ST, wod), (SS, SS.n)])
                                wout_rv_free[dl % 2] = rv
                                s_op(lambda s, d=d, dl=dl, ksl=ksl: s.copy(
                                    out=RES[:, d, ksl], in_=RSCR1[dl % 2]),
                                    wait=[(SV, rv)])
                            woslot_last_use[slot] = ST.n
                        wout_scopy[0] = SS.n
                    return emit_wout
                prev_wout = make_wout()
            prev_wout()
            wslot_last_use[2] = ST.n
            wslot_last_use[0] = max(wslot_last_use[0], ST.n)
            wslot_last_use[1] = max(wslot_last_use[1], ST.n)

        for ph in phases:
            if ph in ("l0f1", "l0f2"):
                emit_ffn(0)
            elif ph in ("l1f1", "l1f2"):
                emit_ffn(1)
            elif ph == "l0a":
                emit_l0attn()
            elif ph == "l1a":
                emit_l1attn()

        fin = ydma(out_h[:, :, 0:T], RES[:, :, :],
                   wait=[(SS, SS.n), (SV, SV.n)])
        emit("y", lambda y: None, wait=[(IO, fin)])

        block = ctx.enter_context(nc.Block())

        @block.tensor
        def _(t):
            for f in Q["t"]:
                f(t)

        @block.vector
        def _(v):
            for f in Q["v"]:
                f(v)

        @block.scalar
        def _(s):
            for f in Q["s"]:
                f(s)

        @block.gpsimd
        def _(g):
            for f in Q["g"]:
                f(g)

        @block.sync
        def _(y):
            for f in Q["y"]:
                f(y)

    return nc



def _lay_feat_major(w_t):
    K, M = w_t.shape
    return np.ascontiguousarray(w_t.reshape(K // 128, 128, M).transpose(1, 0, 2))


def _prep_inputs(inputs, T=2048):
    f = np.float32
    p = {}
    x = np.asarray(inputs["L0_attn_x"], f)
    xs = []
    for i in range(8):
        xi = np.ascontiguousarray(x[i].T)
        xs.append(_lay_feat_major(xi)[:, :, 0:T])
    p["x"] = xs
    p["l0a_wq"] = _lay_feat_major(np.asarray(inputs["L0_attn_Wq"], f).T)
    p["l0a_wk"] = _lay_feat_major(np.asarray(inputs["L0_attn_Wk"], f).T)
    p["l0a_wv"] = _lay_feat_major(np.asarray(inputs["L0_attn_Wv"], f).T)
    p["l0a_wo"] = np.ascontiguousarray(np.asarray(inputs["L0_attn_Wo"], f).T)
    p["l0f_gate"] = _lay_feat_major(np.asarray(inputs["L0_ffn_gate_w"], f).T)
    p["l0f_wg"] = np.stack([_lay_feat_major(np.asarray(inputs["L0_ffn_w_gate"], f)[e].T)
                            for e in range(8)])
    p["l0f_wu"] = np.stack([_lay_feat_major(np.asarray(inputs["L0_ffn_w_up"], f)[e].T)
                            for e in range(8)])
    p["l0f_wd"] = np.stack([_lay_feat_major(np.asarray(inputs["L0_ffn_w_down"], f)[e].T)
                            for e in range(8)])
    p["l1a_win"] = _lay_feat_major(np.asarray(inputs["L1_attn_W_in"], f))
    cw = np.asarray(inputs["L1_attn_conv_w"], f).reshape(2048, 4)
    p["l1a_convw"] = np.ascontiguousarray(cw.reshape(16, 128, 4).transpose(1, 0, 2))
    p["l1a_convb"] = np.ascontiguousarray(
        np.asarray(inputs["L1_attn_conv_b"], f).reshape(16, 128).T)
    p["l1a_wx"] = _lay_feat_major(np.asarray(inputs["L1_attn_W_x"], f))
    p["l1a_wdt"] = np.ascontiguousarray(np.asarray(inputs["L1_attn_W_dt"], f))
    p["l1a_bdt"] = np.ascontiguousarray(
        np.asarray(inputs["L1_attn_b_dt"], f).reshape(16, 128).T)
    p["l1a_alog"] = np.ascontiguousarray(
        np.asarray(inputs["L1_attn_A_log"], f).reshape(16, 128, 16).transpose(1, 0, 2))
    p["l1a_dpar"] = np.ascontiguousarray(
        np.asarray(inputs["L1_attn_D_param"], f).reshape(16, 128).T)
    p["l1a_wout"] = _lay_feat_major(np.asarray(inputs["L1_attn_W_out"], f))
    p["l1f_gate"] = _lay_feat_major(np.asarray(inputs["L1_ffn_gate_w"], f).T)
    p["l1f_cbias"] = np.ascontiguousarray(np.broadcast_to(
        np.asarray(inputs["L1_ffn_correction_bias"], f), (128, 8)))
    p["l1f_wg"] = np.stack([_lay_feat_major(np.asarray(inputs["L1_ffn_Wg"], f)[e])
                            for e in range(8)])
    p["l1f_wu"] = np.stack([_lay_feat_major(np.asarray(inputs["L1_ffn_Wu"], f)[e])
                            for e in range(8)])
    p["l1f_wd"] = np.stack([_lay_feat_major(np.asarray(inputs["L1_ffn_Wd"], f)[e])
                            for e in range(8)])
    p["l1f_wgs"] = _lay_feat_major(np.asarray(inputs["L1_ffn_Wg_s"], f))
    p["l1f_wus"] = _lay_feat_major(np.asarray(inputs["L1_ffn_Wu_s"], f))
    p["l1f_wds"] = _lay_feat_major(np.asarray(inputs["L1_ffn_Wd_s"], f))
    bf = ml_dtypes.bfloat16
    p["c_onescol"] = np.ones((128, 1), bf)
    p["c_onedcol"] = np.full((128, 1), 1.0 / 1024.0, bf)
    p["c_onesrow"] = np.ones((1, 128), bf)
    p["c_idbf"] = np.eye(128, dtype=bf)
    p["c_causalT"] = np.triu(np.ones((128, 128), bf))
    p["c_negeps"] = np.ascontiguousarray(np.broadcast_to(
        (np.arange(8) * 1e-12).astype(f), (128, 8)))
    esel = np.zeros((8, 8 * 128), bf)
    for e in range(8):
        esel[e, e * 128:(e + 1) * 128] = 1.0
    p["c_esel8"] = esel
    nsel = np.zeros((128, 32 * 128), bf)
    for j in range(32):
        nsel[64 + j, j * 128:(j + 1) * 128] = 1.0
    p["c_nsel"] = nsel
    return p


def kernel(**inputs):
    T = 2048
    p = _prep_inputs(inputs, T)
    nc = build_program(T=T)
    shared = {k: v for k, v in p.items() if k != "x"}
    in_maps = [dict(shared, x=p["x"][i]) for i in range(8)]
    res = run_bass_kernel_spmd(nc, in_maps, core_ids=list(range(8)))
    outs = []
    for i in range(8):
        o = np.asarray(res.results[i]["out"], np.float32)
        outs.append(o.transpose(1, 0, 2).reshape(1024, T).T)
    return np.stack(outs).astype(np.float32)
